# version 02
# baseline (speedup 1.0000x reference)
import math
import numpy as np
import ml_dtypes
import concourse.bass as bass
import concourse.mybir as mybir
from concourse.bass_utils import run_bass_kernel_spmd

F32 = mybir.dt.float32
BF16 = mybir.dt.bfloat16
AF = mybir.ActivationFunctionType
ALU = mybir.AluOpType
AX = mybir.AxisListType

S, D, H, DH = 4096, 2048, 16, 128
NCH = D // 128
NTILE = S // 128
NSTRIP = S // 512
I_MOE = 1408
ICH = I_MOE // 128
NG = 4
GT = S // NG
GSTRIPS = GT // 512
GSUB = GT // 128
SCALE = 1.0 / math.sqrt(128.0)
SHIFT = 25.0
EPS = 1e-5
BIS_ITERS = 26

ENGS = ("tensor", "vector", "scalar", "gpsimd", "sync")


class Ctx:
    def __init__(self, nc):
        self.nc = nc
        self.q = {e: [] for e in ENGS}
        self.cnt = {}
        self.sems = {}
        self.lastw = {}

    def sem(self, name):
        h = self.nc.alloc_semaphore(name=name)
        self.sems[name] = h
        self.cnt[name] = 0
        return name

    def emit(self, eng, fn, waits=(), inc=None, amt=1):
        nw = []
        for t in waits:
            if t is None:
                continue
            s, v = t
            assert v <= self.cnt[s], f"forward wait {s} {v} > {self.cnt[s]}"
            if self.lastw.get((eng, s), -1) < v:
                nw.append((self.sems[s], v))
                self.lastw[(eng, s)] = v
        tok = None
        semh = None
        if inc is not None:
            self.cnt[inc] += amt
            tok = (inc, self.cnt[inc])
            semh = self.sems[inc]

        def play(e, nw=nw, fn=fn, semh=semh, amt=amt):
            for (hh, v) in nw:
                e.wait_ge(hh, v)
            ins = fn(e)
            if semh is not None:
                ins.then_inc(semh, amt)
        self.q[eng].append(play)
        return tok

    def wait(self, eng, waits):
        self.emit(eng, lambda e: e.nop(), waits=waits)

    def dma(self, eng, out, in_, sem, waits=()):
        waits = list(waits) + [(sem, self.cnt[sem])]
        return self.emit(eng, lambda e, o=out, i=in_: e.dma_start(out=o, in_=i),
                         waits=waits, inc=sem, amt=16)

    def tok(self, sem):
        return (sem, self.cnt[sem])


def build(debug=False):
    nc = bass.Bass("TRN2", target_bir_lowering=False)

    x_d = nc.dram_tensor("x", [S, D], F32, kind="ExternalInput").ap()
    wqT_d = nc.dram_tensor("wqT", [H, D, DH], F32, kind="ExternalInput").ap()
    wkT_d = nc.dram_tensor("wkT", [H, D, DH], F32, kind="ExternalInput").ap()
    wvT_d = nc.dram_tensor("wvT", [H, D, DH], F32, kind="ExternalInput").ap()
    woT_d = nc.dram_tensor("woT", [D, D], F32, kind="ExternalInput").ap()
    gwT_d = nc.dram_tensor("gwT", [D, 16], F32, kind="ExternalInput").ap()
    wg_d = nc.dram_tensor("wg", [16, D, I_MOE], F32, kind="ExternalInput").ap()
    wu_d = nc.dram_tensor("wu", [16, D, I_MOE], F32, kind="ExternalInput").ap()
    wd_d = nc.dram_tensor("wd", [16, I_MOE, D], F32, kind="ExternalInput").ap()
    swg_d = nc.dram_tensor("swg", [D, 2 * I_MOE], F32, kind="ExternalInput").ap()
    swu_d = nc.dram_tensor("swu", [D, 2 * I_MOE], F32, kind="ExternalInput").ap()
    swd_d = nc.dram_tensor("swd", [2 * I_MOE, D], F32, kind="ExternalInput").ap()
    masks_d = nc.dram_tensor("masks", [128, 4 * 512], BF16, kind="ExternalInput").ap()
    ident_d = nc.dram_tensor("ident", [128, 128], BF16, kind="ExternalInput").ap()
    onesb_d = nc.dram_tensor("onesb", [128, 1], BF16, kind="ExternalInput").ap()
    rowones_d = nc.dram_tensor("rowones", [1, 128], F32, kind="ExternalInput").ap()
    ccols_d = nc.dram_tensor("ccols", [128, 2], F32, kind="ExternalInput").ap()

    out_d = nc.dram_tensor("out", [NTILE, 128, D], F32, kind="ExternalOutput").ap()

    x2s_d = nc.dram_tensor("x2s", [NTILE, 128, D], F32, kind="Internal").ap()
    ln2Ts_d = nc.dram_tensor("ln2Ts", [NSTRIP, NCH, 128, 512], BF16, kind="Internal").ap()
    zs_d = nc.dram_tensor("zs", [H, 128, S], BF16, kind="Internal").ap()
    wgb_d = nc.dram_tensor("wgb", [16, D, I_MOE], BF16, kind="Internal").ap()
    wub_d = nc.dram_tensor("wub", [16, D, I_MOE], BF16, kind="Internal").ap()
    wdb_d = nc.dram_tensor("wdb", [16, I_MOE, D], BF16, kind="Internal").ap()
    swgb_d = nc.dram_tensor("swgb", [D, 2 * I_MOE], BF16, kind="Internal").ap()
    swub_d = nc.dram_tensor("swub", [D, 2 * I_MOE], BF16, kind="Internal").ap()
    swdb_d = nc.dram_tensor("swdb", [2 * I_MOE, D], BF16, kind="Internal").ap()

    dbg = {}
    if debug:
        dbg["ln1T"] = nc.dram_tensor("dbg_ln1T", [128, NCH * S], BF16, kind="ExternalOutput").ap()
        dbg["z"] = nc.dram_tensor("dbg_z", [H, 128, S], BF16, kind="ExternalOutput").ap()
        dbg["x2"] = nc.dram_tensor("dbg_x2", [NTILE, 128, D], F32, kind="ExternalOutput").ap()
        dbg["comb"] = nc.dram_tensor("dbg_comb", [128, NTILE * 16], F32, kind="ExternalOutput").ap()
        dbg["ln2T"] = nc.dram_tensor("dbg_ln2T", [NSTRIP, NCH, 128, 512], BF16, kind="ExternalOutput").ap()

    A = nc.alloc_sbuf_tensor
    ident = A("c_ident", [128, 128], BF16)
    masks = A("c_masks", [128, 4 * 512], BF16)
    onesb = A("c_onesb", [128, 1], BF16)
    rowones = A("c_rowones", [1, 128], F32)
    ccols = A("c_ccols", [128, 2], F32)
    gwT_sb = A("gwT_sb", [128, 16 * 16], BF16)
    comb = A("comb", [128, NTILE * 16], F32)
    probs = A("probs", [128, NTILE * 16], F32)
    stats = A("stats", [128, 16], F32)
    rsum = A("rsum", [1, 512], F32)
    eps_c = ccols[:, 0:1]
    shm_c = ccols[:, 1:2]

    A_BYTES = 131072
    B_BYTES = 68000
    top = nc.sbuf_top
    base_A = top - A_BYTES
    base_B = base_A - B_BYTES
    assert nc.sbuf_base + 64 < base_B, (nc.sbuf_base, base_B)

    def at(name, shape, dtype, off):
        nbytes = shape[1] * (2 if dtype == BF16 else 4)
        assert off + nbytes <= top, (name, off, nbytes)
        return nc.alloc_sbuf_tensor_at(name, shape, dtype, offset=off)

    ln1T = at("ln1T", [128, NCH * S], BF16, base_A)
    woT_sb = at("woT_sb", [128, NCH * D], BF16, base_A)
    zsb = [at(f"zsb{i}", [128, H * 512], BF16, base_A + 65536 + i * 16384) for i in range(2)]
    ao = at("ao", [128, D], F32, base_A + 98304)
    x2st = at("x2st", [128, D], F32, base_A + 106496)
    yg = at("yg", [128, GSUB * D], F32, base_A)
    lnTg = at("lnTg", [128, GSTRIPS * NCH * 512], BF16, base_A + 65536)
    ae = at("ae", [128, ICH * GT], BF16, base_A + 98304)
    dump3 = at("dump3", [128, D], F32, base_A + 120832)

    xt = [at(f"xt{i}", [128, D], F32, base_B + i * 8192) for i in range(2)]
    dump0 = at("dump0", [128, D], F32, base_B + 32768)
    dump2 = at("dump2", [128, D], F32, base_B + 43008)
    ln1b = at("ln1b", [128, D], BF16, base_B + 24576)
    wqkv = [at(f"wqkv{i}", [128, 3 * NCH * DH], BF16, base_B + i * 12288) for i in range(2)]
    Qf = at("Qf", [128, S], BF16, base_B + 24576)
    Kf = at("Kf", [128, S], BF16, base_B + 32768)
    Vt = at("Vt", [128, S], BF16, base_B + 40960)
    vtmp = at("vtmp", [128, 512], BF16, base_B + 49152)
    zh = at("zh", [128, S], BF16, base_B + 50176)
    atb = [at(f"atb{i}", [128, 512], BF16, base_B + 58368 + i * 1024) for i in range(3)]
    bc_sb = at("bc_sb", [128, 512], F32, base_B + 61440)
    xin = [at(f"xin{i}", [128, D], F32, base_B + i * 8192) for i in range(2)]
    ln2b = at("ln2b", [128, D], BF16, base_B + 16384)
    l2st = at("l2st", [128, NCH * 512], BF16, base_B + 20480)
    ptmp = at("ptmp", [128, 16], F32, base_B + 36864)
    gst = at("gst", [128, 7 * NTILE], F32, base_B)
    cmpa = at("cmpa", [128, NTILE * 16], F32, base_B + 896)
    wgq = [at(f"wgq{i}", [128, NCH * 128], BF16, base_B + i * 4096) for i in range(3)]
    wuq = [at(f"wuq{i}", [128, NCH * 128], BF16, base_B + 12288 + i * 4096) for i in range(3)]
    wdq = [at(f"wdq{i}", [128, ICH * 512], BF16, base_B + 24576 + i * 11264) for i in range(2)]
    sgb = [at(f"sgb{i}", [128, 512], F32, base_B + 47104 + i * 2048) for i in range(2)]
    xf = [at(f"xf{i}", [128, D], F32, base_B + 51200) for i in range(1)]
    outf = at("outf", [128, D], F32, base_B + 59392)

    ps_a = nc.alloc_psum_tensor("ps_a", [128, 512], F32)
    ps_b = nc.alloc_psum_tensor("ps_b", [128, 512], F32)
    ps_c = nc.alloc_psum_tensor("ps_c", [128, 512], F32)
    ps_d = nc.alloc_psum_tensor("ps_d", [128, 512], F32)
    ps_sum = nc.alloc_psum_tensor("ps_sum", [1, 512], F32)
    ps_bc = nc.alloc_psum_tensor("ps_bc", [128, 512], F32)
    ps_t1 = nc.alloc_psum_tensor("ps_t1", [128, 1024], BF16)
    ps_t2 = nc.alloc_psum_tensor("ps_t2", [128, 1024], BF16)

    c = Ctx(nc)
    s_pe = c.sem("s_pe")
    s_act = c.sem("s_act")
    s_dve = c.sem("s_dve")
    s_dvs = c.sem("s_dvs")
    s_acs = c.sem("s_acs")
    s_ci = c.sem("s_ci")
    s_gw = c.sem("s_gw")
    s_x = [c.sem("s_x0"), c.sem("s_x1")]
    s_wq = [c.sem("s_wq0"), c.sem("s_wq1")]
    s_zw = c.sem("s_zw")
    s_zs = [c.sem("s_zs0"), c.sem("s_zs1")]
    s_x2w = c.sem("s_x2w")
    s_l2w = c.sem("s_l2w")
    s_wo = c.sem("s_wo")
    s_lng = c.sem("s_lng")
    s_wg = [c.sem("s_wg0"), c.sem("s_wg1"), c.sem("s_wg2")]
    s_wu = [c.sem("s_wu0"), c.sem("s_wu1"), c.sem("s_wu2")]
    s_wd = [c.sem("s_wd0"), c.sem("s_wd1")]
    s_wgc = [c.sem("s_wgc0"), c.sem("s_wgc1"), c.sem("s_wgc2")]
    s_wuc = [c.sem("s_wuc0"), c.sem("s_wuc1"), c.sem("s_wuc2")]
    s_wdc = [c.sem("s_wdc0"), c.sem("s_wdc1")]
    s_sg = [c.sem("s_sg0"), c.sem("s_sg1"), c.sem("s_sg2")]
    s_su = [c.sem("s_su0"), c.sem("s_su1"), c.sem("s_su2")]
    s_sd = [c.sem("s_sd0"), c.sem("s_sd1")]
    s_xf = [c.sem("s_xf0"), c.sem("s_xf1")]
    s_ow = c.sem("s_ow")
    s_dbg = c.sem("s_dbg")

    ln1T_r = ln1T[:].rearrange("p (c t) -> p c t", c=NCH)

    c.dma("sync", ident[:], ident_d, s_ci)
    c.dma("sync", masks[:], masks_d, s_ci)
    c.dma("sync", onesb[:], onesb_d, s_ci)
    c.dma("sync", rowones[:], rowones_d, s_ci)
    c.dma("sync", ccols[:], ccols_d, s_ci)
    c.dma("gpsimd", gwT_sb[:].rearrange("p (c e) -> p c e", c=NCH),
          gwT_d.rearrange("(c p) e -> p c e", p=128), s_gw)
    t_init_in = c.tok(s_ci)
    t_init_cst = c.tok(s_gw)

    def ln_stats(src_ap, waits, dmp, cb=0):
        sc = lambda k: stats[:, cb + k:cb + k + 1]
        t1 = c.emit("scalar", lambda e, a=src_ap, d=dmp: e.activation(
            d[:], a, AF.Identity, accum_out=sc(0)), waits=waits, inc=s_act)
        t2 = c.emit("scalar", lambda e, a=src_ap, d=dmp: e.activation(
            d[:], a, AF.Square, accum_out=sc(1)), waits=[t1], inc=s_act)
        c.emit("vector", lambda e: e.tensor_scalar_mul(sc(2), sc(0), 1.0 / D),
               waits=[t2], inc=s_dve)
        t3 = c.emit("vector", lambda e: e.tensor_scalar_mul(sc(3), sc(1), 1.0 / D),
                    inc=s_dvs)
        t4 = c.emit("vector", lambda e: e.tensor_tensor(sc(4), sc(2), sc(2), ALU.mult),
                    waits=[t3], inc=s_dvs)
        t5 = c.emit("vector", lambda e: e.tensor_tensor(sc(3), sc(3), sc(4), ALU.subtract),
                    waits=[t4], inc=s_dve)
        t6 = c.emit("scalar", lambda e: e.activation(sc(5), sc(3), AF.Ln, bias=eps_c),
                    waits=[t5], inc=s_acs)
        t7 = c.emit("scalar", lambda e: e.activation(sc(5), sc(5), AF.Exp, scale=-0.5),
                    waits=[t6], inc=s_act)
        return t7

    t_apply = {}
    t_evac = {}
    t_trdone = {}
    t_xload = [None, None]
    for i in range(NTILE):
        sl = i % 2
        w = [t_apply.get(i - 2)]
        t_xload[sl] = c.dma("sync", xt[sl][:], x_d[i * 128:(i + 1) * 128, :], s_x[sl], waits=w)
        t_r = ln_stats(xt[sl][:], [t_xload[sl], t_init_in], dump0)
        w = [t_r]
        if i >= 1:
            w.append(t_trdone[2 * i - 1])
        t_apply[i] = c.emit("vector", lambda e, sl=sl: e.tensor_scalar(
            ln1b[:], xt[sl][:], stats[:, 2:3], stats[:, 5:6], ALU.subtract, ALU.mult),
            waits=w, inc=s_dve)
        for half in range(2):
            bank = ps_t1 if (i * 2 + half) % 2 == 0 else ps_t2
            w = [t_apply[i], t_evac.get((i * 2 + half) - 2)]
            for k in range(8):
                ch = half * 8 + k
                c.emit("tensor", lambda e, b=bank, k=k, ch=ch: e.transpose(
                    b[:, k * 128:(k + 1) * 128], ln1b[:, ch * 128:(ch + 1) * 128], ident[:]),
                    waits=w if k == 0 else [], inc=(s_pe if k == 7 else None))
            t_trdone[i * 2 + half] = c.tok(s_pe)
            dst = ln1T_r[:, half * 8:(half + 1) * 8, i * 128:(i + 1) * 128]
            t_evac[i * 2 + half] = c.emit(
                "vector", lambda e, b=bank, d=dst: e.tensor_copy(d, b[:]),
                waits=[t_trdone[i * 2 + half]], inc=s_dve)
    t_ln1T_done = c.tok(s_dve)

    if debug:
        c.dma("scalar", dbg["ln1T"], ln1T[:], s_dbg, waits=[t_ln1T_done])

    t_qkv_used = {}
    t_zdma = {}
    t_zmul = {}
    t_recip = {}
    t_bccopy = {}

    for h in range(H):
        sl = h % 2
        w = [t_ln1T_done if h == 0 else None, t_qkv_used.get(h - 2)]
        buf = wqkv[sl]
        t_w = None
        for mi, wsrc in enumerate((wqT_d, wkT_d, wvT_d)):
            src = wsrc[h].rearrange("(c p) m -> p c m", p=128)
            dst = buf[:, mi * 2048:(mi + 1) * 2048].rearrange("p (c m) -> p c m", c=NCH)
            t_w = c.dma("gpsimd", dst, src, s_wq[sl],
                        waits=w if mi == 0 else [])
        t_vt_done = None
        t_qkv_evac = {}
        for mi in range(3):
            dstt = (Qf, Kf, None)[mi]
            for s in range(NSTRIP):
                bank = ps_a if s % 2 == 0 else ps_b
                w = [t_w]
                if h > 0 and mi == 0 and s == 0:
                    w.append(t_zmul[(h - 1, NSTRIP - 1)])
                for ci in range(NCH):
                    c.emit("tensor", lambda e, b=bank, buf=buf, mi=mi, ci=ci, s=s: e.matmul(
                        b[:], buf[:, mi * 2048 + ci * 128:mi * 2048 + (ci + 1) * 128],
                        ln1T[:, ci * S + s * 512:ci * S + (s + 1) * 512],
                        start=(ci == 0), stop=(ci == NCH - 1)),
                        waits=w if ci == 0 else [],
                        inc=(s_pe if ci == NCH - 1 else None))
                t_mm = c.tok(s_pe)
                if mi < 2:
                    t_qkv_evac[(mi, s)] = c.emit(
                        "scalar", lambda e, d=dstt, b=bank, s=s: e.copy(
                            d[:, s * 512:(s + 1) * 512], b[:]),
                        waits=[t_mm], inc=s_act)
                else:
                    t_ev = c.emit("scalar", lambda e, b=bank: e.copy(vtmp[:], b[:]),
                                  waits=[t_mm], inc=s_act)
                    tb = ps_t1 if s % 2 == 0 else ps_t2
                    for k in range(4):
                        c.emit("tensor", lambda e, tb=tb, k=k: e.transpose(
                            tb[:, k * 128:(k + 1) * 128], vtmp[:, k * 128:(k + 1) * 128], ident[:]),
                            waits=[t_ev] if k == 0 else [], inc=(s_pe if k == 3 else None))
                    t_tr = c.tok(s_pe)
                    t_vt_done = c.emit("vector", lambda e, tb=tb, s=s: e.tensor_copy(
                        Vt[:, s * 512:(s + 1) * 512], tb[:, 0:512]),
                        waits=[t_tr], inc=s_dve)
        t_qkv_used[h] = c.tok(s_pe)
        t_k_done = t_qkv_evac[(1, NSTRIP - 1)]
        t_q_done = t_qkv_evac[(0, NSTRIP - 1)]

        for qs in range(NSTRIP):
            npc = 4 * qs + 4
            zbank = ps_c if qs % 2 == 0 else ps_d
            zoth = ps_d if qs % 2 == 0 else ps_c
            sring = [ps_a, ps_b, zoth]
            t_exp = {}
            t_scores = {}
            last_avstop = {}

            def emit_score(pc, qs=qs, npc=npc, sring=sring):
                bank = sring[pc % 3]
                w = [t_k_done, t_q_done, t_exp.get(pc - 3)]
                if qs == 0 and pc <= 1:
                    w.append(t_vt_done)
                if pc % 3 == 2 and pc == 2:
                    pk = (h, qs - 1) if qs > 0 else (h - 1, NSTRIP - 1)
                    if pk in t_zmul:
                        w.append(t_zmul[pk])
                t_scores[pc] = c.emit("tensor", lambda e, b=bank, pc=pc, qs=qs: e.matmul(
                    b[:], Kf[:, pc * 128:(pc + 1) * 128], Qf[:, qs * 512:(qs + 1) * 512],
                    start=True, stop=True), waits=w, inc=s_pe)

            emit_score(0)
            if npc > 1:
                emit_score(1)
            for pc in range(npc):
                bank = sring[pc % 3]
                wexp = [t_scores[pc], last_avstop.get(pc - 3)]
                if pc >= 4 * qs:
                    j = pc - 4 * qs
                    t_msk = c.emit("vector", lambda e, b=bank, j=j: e.tensor_tensor(
                        b[:], b[:], masks[:, j * 512:(j + 1) * 512], ALU.add),
                        waits=[t_scores[pc]], inc=s_dve)
                    wexp[0] = t_msk
                ab = atb[pc % 3]
                t_exp[pc] = c.emit("scalar", lambda e, ab=ab, b=bank: e.activation(
                    ab[:], b[:], AF.Exp, bias=shm_c, scale=SCALE),
                    waits=wexp, inc=s_act)
                if pc + 2 < npc:
                    emit_score(pc + 2)
                w_s = [t_exp[pc]]
                if pc == 0 and (h > 0 or qs > 0):
                    w_s.append(t_recip[(h, qs - 1) if qs > 0 else (h - 1, NSTRIP - 1)])
                c.emit("tensor", lambda e, ab=ab, pc=pc, npc=npc: e.matmul(
                    ps_sum[:], onesb[:], ab[:], start=(pc == 0), stop=(pc == npc - 1)),
                    waits=w_s, inc=(s_pe if pc == npc - 1 else None))
                w_z = [t_vt_done]
                if pc == 0 and (h * NSTRIP + qs) >= 2:
                    pqs = qs - 2 if qs >= 2 else qs + NSTRIP - 2
                    ph = h if qs >= 2 else h - 1
                    w_z.append(t_zmul[(ph, pqs)])
                last_avstop[pc] = c.emit("tensor", lambda e, ab=ab, zb=zbank, pc=pc, npc=npc:
                                         e.matmul(zb[:], Vt[:, pc * 128:(pc + 1) * 128], ab[:],
                                                  start=(pc == 0), stop=(pc == npc - 1)),
                                         waits=w_z, inc=s_pe)
            t_sum_stop = (s_pe, c.cnt[s_pe] - 1)
            t_av_stop = last_avstop[npc - 1]
            t_recip[(h, qs)] = c.emit("vector", lambda e: e.reciprocal(rsum[:], ps_sum[:]),
                                      waits=[t_sum_stop], inc=s_dve)
            w_bc = [t_recip[(h, qs)], t_init_in]
            if (h, qs) != (0, 0):
                pk = (h, qs - 1) if qs > 0 else (h - 1, NSTRIP - 1)
                w_bc += [t_bccopy[pk], t_zmul[pk]]
            t_bcmm = c.emit("tensor", lambda e: e.matmul(ps_bc[:], rowones[:], rsum[:],
                                                         start=True, stop=True),
                            waits=w_bc, inc=s_pe)
            t_bccopy[(h, qs)] = c.emit("scalar", lambda e: e.copy(bc_sb[:], ps_bc[:]),
                                       waits=[t_bcmm], inc=s_act)
            w_zm = [t_bccopy[(h, qs)], t_av_stop]
            if h > 0 and qs == 0:
                w_zm.append(t_zdma[h - 1])
            t_zmul[(h, qs)] = c.emit(
                "vector", lambda e, qs=qs, zb=zbank:
                e.tensor_tensor(zh[:, qs * 512:(qs + 1) * 512], zb[:], bc_sb[:], ALU.mult),
                waits=w_zm, inc=s_dve)
        t_zdma[h] = c.dma("scalar", zs_d[h], zh[:], s_zw, waits=[t_zmul[(h, NSTRIP - 1)]])

    t_p1_pe_done = c.tok(s_pe)
    t_p1_dve_done = c.tok(s_dve)
    t_zdma_all = c.tok(s_zw)

    if debug:
        c.dma("scalar", dbg["z"], zs_d, s_dbg, waits=[t_zdma_all])

    t_wo = c.dma("gpsimd", woT_sb[:].rearrange("p (c d) -> p c d", c=NCH),
                 woT_d.rearrange("(c p) d -> p c d", p=128),
                 s_wo, waits=[t_p1_pe_done])

    t_zs_used = {}
    t_zsl = [None, None]
    t_x2add = {}
    t_x2w_l = {}
    t_gexp = {}
    t_probs = {}
    t_l2w = {}
    t_l2ev = {}
    t_l2b_prev = [None]
    t_tr2done_l = [None]
    zs_r = zs_d.rearrange("h p (s f) -> s p h f", s=NSTRIP)

    deferred = [None]

    def emit_deferred():
        if deferred[0] is None:
            return
        deferred[0]()
        deferred[0] = None

    for t in range(NTILE):
        st = t // 4
        sl = st % 2
        sl2 = t % 2
        if t % 4 == 0:
            w = [t_zdma_all, t_p1_dve_done, t_zs_used.get(st - 2)]
            t_zsl[sl] = c.dma("sync", zsb[sl][:].rearrange("p (h f) -> p h f", h=H),
                              zs_r[st], s_zs[sl], waits=w)
        t_xl = c.dma("sync", xin[sl2][:], x_d[t * 128:(t + 1) * 128, :], s_x[sl2],
                     waits=[t_p1_dve_done, t_x2add.get(t - 2)])
        t_wod = {}
        for dd in range(4):
            bank = ps_a if dd % 2 == 0 else ps_b
            w = [t_wo, t_zsl[sl], t_x2add.get(t - 1)]
            for ci in range(NCH):
                c.emit("tensor", lambda e, b=bank, ci=ci, dd=dd, t=t, sl=sl: e.matmul(
                    b[:], zsb[sl][:, ci * 512 + (t % 4) * 128:ci * 512 + (t % 4) * 128 + 128],
                    woT_sb[:, ci * D + dd * 512:ci * D + dd * 512 + 512],
                    start=(ci == 0), stop=(ci == NCH - 1)),
                    waits=w if ci == 0 else [], inc=(s_pe if ci == NCH - 1 else None))
            t_mm = c.tok(s_pe)
            t_wod[dd] = c.emit("scalar", lambda e, b=bank, dd=dd: e.copy(
                ao[:, dd * 512:(dd + 1) * 512], b[:]), waits=[t_mm], inc=s_act)
            if dd == 1:
                emit_deferred()
        t_zs_used[st] = c.tok(s_pe)
        t_r = ln_stats(ao[:], [t_wod[3]], dump2)
        t_norm = c.emit("vector", lambda e: e.tensor_scalar(
            ao[:], ao[:], stats[:, 2:3], stats[:, 5:6], ALU.subtract, ALU.mult),
            waits=[t_r], inc=s_dvs)
        t_x2add[t] = c.emit("vector", lambda e, sl2=sl2: e.tensor_tensor(
            x2st[:], ao[:], xin[sl2][:], ALU.add),
            waits=[t_norm, t_xl, t_x2w_l.get(t - 1), t_l2b_prev[0]], inc=s_dve)
        t_x2w_l[t] = c.dma("scalar", x2s_d[t], x2st[:], s_x2w, waits=[t_x2add[t]])
        t_r2 = ln_stats(x2st[:], [t_x2add[t], t_l2b_prev[0]], dump2, cb=8)

        def make_deferred(t=t, st=st, t_r2=t_r2):
            def go():
                t_l2b = c.emit("vector", lambda e: e.tensor_scalar(
                    ln2b[:], x2st[:], stats[:, 10:11], stats[:, 13:14],
                    ALU.subtract, ALU.mult),
                    waits=[t_r2, t_tr2done_l[0]], inc=s_dve)
                t_l2b_prev[0] = t_l2b
                for half in range(2):
                    bank = ps_t1 if half == 0 else ps_t2
                    w = [t_l2b, t_l2ev.get(half)]
                    for k in range(8):
                        ch = half * 8 + k
                        c.emit("tensor", lambda e, b=bank, k=k, ch=ch: e.transpose(
                            b[:, k * 128:(k + 1) * 128], ln2b[:, ch * 128:(ch + 1) * 128],
                            ident[:]),
                            waits=w if k == 0 else [], inc=(s_pe if k == 7 else None))
                    t_trh = c.tok(s_pe)
                    if half == 1:
                        t_tr2done_l[0] = t_trh
                    dst = l2st[:].rearrange("p (c f) -> p c f", c=NCH)[
                        :, half * 8:(half + 1) * 8, (t % 4) * 128:(t % 4) * 128 + 128]
                    wv = [t_trh]
                    if st > 0 and t % 4 == 0:
                        wv.append(t_l2w[st - 1])
                    t_l2ev[half] = c.emit("vector", lambda e, b=bank, d=dst: e.tensor_copy(d, b[:]),
                                          waits=wv, inc=s_dve)
                w = [t_l2ev[0], t_l2ev[1], t_init_cst, t_gexp.get(t - 1)]
                for ci in range(NCH):
                    c.emit("tensor", lambda e, ci=ci, t=t: e.matmul(
                        ps_bc[:, 0:16],
                        l2st[:, ci * 512 + (t % 4) * 128:ci * 512 + (t % 4) * 128 + 128],
                        gwT_sb[:, ci * 16:(ci + 1) * 16],
                        start=(ci == 0), stop=(ci == NCH - 1)),
                        waits=w if ci == 0 else [], inc=(s_pe if ci == NCH - 1 else None))
                t_gmm = c.tok(s_pe)
                t_nm = c.emit("vector", lambda e: e.reduce_max(stats[:, 6:7], ps_bc[:, 0:16],
                                                               axis=AX.X, negate=True),
                              waits=[t_gmm], inc=s_dve)
                t_gexp[t] = c.emit("scalar", lambda e: e.activation(ptmp[:], ps_bc[:, 0:16],
                                                                    AF.Exp, bias=stats[:, 6:7]),
                                   waits=[t_nm], inc=s_act)
                t_rs = c.emit("vector", lambda e: e.reduce_sum(stats[:, 7:8], ptmp[:], axis=AX.X),
                              waits=[t_gexp[t]], inc=s_dvs)
                t_rw = c.emit("vector", lambda e: e.reciprocal(stats[:, 14:15], stats[:, 7:8]),
                              waits=[t_rs], inc=s_dvs)
                t_probs[t] = c.emit("vector", lambda e, t=t: e.tensor_scalar_mul(
                    probs[:, t * 16:(t + 1) * 16], ptmp[:], stats[:, 14:15]),
                    waits=[t_rw], inc=s_dve)
                if t % 4 == 3:
                    t_l2w[st] = c.dma("scalar", ln2Ts_d[st].rearrange("c p f -> p c f"),
                                      l2st[:].rearrange("p (c f) -> p c f", c=NCH),
                                      s_l2w, waits=[t_l2ev[0], t_l2ev[1]])
            return go
        deferred[0] = make_deferred()
    emit_deferred()

    LO, HI, CNT, PRD, MID, D1, D2 = range(7)
    col = lambda k: gst[:, k * NTILE:(k + 1) * NTILE]
    c.emit("vector", lambda e: e.memset(col(LO), 0.0),
           waits=[t_probs[NTILE - 1]], inc=s_dve)
    for t in range(NTILE):
        c.emit("vector", lambda e, t=t: e.reduce_max(
            gst[:, HI * NTILE + t:HI * NTILE + t + 1],
            probs[:, t * 16:(t + 1) * 16], axis=AX.X),
            inc=(s_dvs if t == NTILE - 1 else None))
    t_bis = c.tok(s_dvs)
    for it in range(BIS_ITERS):
        t_m1 = c.emit("vector", lambda e: e.tensor_tensor(col(MID), col(LO), col(HI), ALU.add),
                      waits=[t_bis], inc=s_dvs)
        t_m2 = c.emit("vector", lambda e: e.tensor_scalar_mul(col(MID), col(MID), 0.5),
                      waits=[t_m1], inc=s_dvs)
        for t in range(NTILE):
            c.emit("vector", lambda e, t=t: e.tensor_scalar(
                cmpa[:, t * 16:(t + 1) * 16], probs[:, t * 16:(t + 1) * 16],
                gst[:, MID * NTILE + t:MID * NTILE + t + 1], None, ALU.is_ge),
                waits=[t_m2] if t == 0 else [], inc=(s_dvs if t == NTILE - 1 else None))
        t_cmp = c.tok(s_dvs)
        for t in range(NTILE):
            c.emit("vector", lambda e, t=t: e.reduce_sum(
                gst[:, CNT * NTILE + t:CNT * NTILE + t + 1],
                cmpa[:, t * 16:(t + 1) * 16], axis=AX.X),
                waits=[t_cmp] if t == 0 else [], inc=(s_dvs if t == NTILE - 1 else None))
        t_cnt = c.tok(s_dvs)
        t_prd = c.emit("vector", lambda e: e.tensor_scalar(col(PRD), col(CNT), 4.0, None, ALU.is_ge),
                       waits=[t_cnt], inc=s_dvs)
        c.emit("vector", lambda e: e.tensor_tensor(col(D1), col(MID), col(LO), ALU.subtract),
               waits=[t_prd])
        t_d2 = c.emit("vector", lambda e: e.tensor_tensor(col(D2), col(HI), col(MID), ALU.subtract),
                      inc=s_dvs)
        c.emit("vector", lambda e: e.tensor_tensor(col(D1), col(D1), col(PRD), ALU.mult),
               waits=[t_d2])
        t_m3 = c.emit("vector", lambda e: e.tensor_tensor(col(D2), col(D2), col(PRD), ALU.mult),
                      inc=s_dvs)
        c.emit("vector", lambda e: e.tensor_tensor(col(LO), col(LO), col(D1), ALU.add),
               waits=[t_m3])
        t_bis = c.emit("vector", lambda e: e.tensor_tensor(col(HI), col(MID), col(D2), ALU.add),
                       inc=s_dvs)
    for t in range(NTILE):
        c.emit("vector", lambda e, t=t: e.tensor_scalar(
            cmpa[:, t * 16:(t + 1) * 16], probs[:, t * 16:(t + 1) * 16],
            gst[:, LO * NTILE + t:LO * NTILE + t + 1], None, ALU.is_ge),
            waits=[t_bis] if t == 0 else [], inc=(s_dvs if t == NTILE - 1 else None))
    t_c1 = c.tok(s_dvs)
    for t in range(NTILE):
        c.emit("vector", lambda e, t=t: e.tensor_tensor(
            cmpa[:, t * 16:(t + 1) * 16], cmpa[:, t * 16:(t + 1) * 16],
            probs[:, t * 16:(t + 1) * 16], ALU.mult),
            waits=[t_c1] if t == 0 else [], inc=(s_dvs if t == NTILE - 1 else None))
    t_c2 = c.tok(s_dvs)
    for t in range(NTILE):
        c.emit("vector", lambda e, t=t: e.reduce_sum(
            gst[:, CNT * NTILE + t:CNT * NTILE + t + 1],
            cmpa[:, t * 16:(t + 1) * 16], axis=AX.X),
            waits=[t_c2] if t == 0 else [], inc=(s_dvs if t == NTILE - 1 else None))
    t_c3 = c.tok(s_dvs)
    t_c4 = c.emit("vector", lambda e: e.reciprocal(col(PRD), col(CNT)), waits=[t_c3], inc=s_dvs)
    for t in range(NTILE):
        c.emit("vector", lambda e, t=t: e.tensor_scalar_mul(
            comb[:, t * 16:(t + 1) * 16], cmpa[:, t * 16:(t + 1) * 16],
            gst[:, PRD * NTILE + t:PRD * NTILE + t + 1]),
            waits=[t_c4] if t == 0 else [], inc=(s_dve if t == NTILE - 1 else None))
    t_comb_done = c.tok(s_dve)

    if debug:
        c.dma("scalar", dbg["x2"], x2s_d, s_dbg, waits=[(s_x2w, 16 * NTILE)])
        c.dma("scalar", dbg["comb"], comb[:], s_dbg, waits=[t_comb_done])
        c.dma("scalar", dbg["ln2T"], ln2Ts_d, s_dbg, waits=[(s_l2w, 16 * NSTRIP)])

    def esrc(e):
        if e < 16:
            return (wg_d[e], wu_d[e], wd_d[e], wgb_d[e], wub_d[e], wdb_d[e], e)
        hh = e - 16
        cs = slice(hh * I_MOE, (hh + 1) * I_MOE)
        return (swg_d[:, cs], swu_d[:, cs], swd_d[cs, :],
                swgb_d[:, cs], swub_d[:, cs], swdb_d[cs, :], None)

    t_p2_pe_done = c.tok(s_pe)
    t_l2w_all = (s_l2w, 16 * NSTRIP)
    rr = lambda ap: ap.rearrange("(c p) i -> p c i", p=128)
    rr_s = lambda t: t[:].rearrange("p (c i) -> p c i", c=NCH)

    t_gu_pe_done = None
    t_fin_done = None
    t_spills_g0 = None
    t_fin_roll = [None, None]
    t_out_roll = None
    n_out = 0
    t_outw = {}
    t_prev_u2 = None
    t_prev_silu2 = None
    t_prev_mult2 = None
    t_prev_d2 = None
    t_prev_ya_last = None
    t_wspill_g = {0: None, 1: None, 2: None}
    t_wspill_u = {0: None, 1: None, 2: None}
    t_dspill = {0: None, 1: None}
    t_slot_used = {0: None, 1: None, 2: None}
    gchunk = [0]

    for g in range(NG):
        w = [t_l2w_all, t_comb_done, t_p2_pe_done, t_gu_pe_done]
        src = ln2Ts_d[2 * g:2 * g + 2].rearrange("s c p f -> p s c f")
        t_lng = c.dma("sync", lnTg[:].rearrange("p (s c f) -> p s c f", s=GSTRIPS, c=NCH),
                      src, s_lng, waits=w)
        eorder = [16, 17] + list(range(16))
        for ei, e_ in enumerate(eorder):
            gsrc, usrc, dsrc, gbf, ubf, dbf, ce = esrc(e_)
            t_udone = {}
            t_silu_l = {}
            t_mult_l = {}
            nit = 0
            for ic in range(ICH):
                sl = gchunk[0] % 3
                gchunk[0] += 1
                w_ld = [t_slot_used[sl], t_wspill_g[sl], t_wspill_u[sl]]
                if (g, ei) == (0, 0) and ic < 2:
                    w_ld += [t_comb_done, t_l2w_all]
                if g == 1 and ei == 0 and ic < 2:
                    w_ld += t_spills_g0
                csl = slice(ic * 128, (ic + 1) * 128)
                if g == 0:
                    t_gl = c.dma("gpsimd", rr_s(wgq[sl]), rr(gsrc[:, csl]), s_wgc[sl], waits=w_ld)
                    t_ul = c.dma("gpsimd", rr_s(wuq[sl]), rr(usrc[:, csl]), s_wuc[sl])
                    t_wspill_g[sl] = c.dma("scalar", rr(gbf[:, csl]), rr_s(wgq[sl]), s_sg[sl],
                                           waits=[t_gl])
                    t_wspill_u[sl] = c.dma("scalar", rr(ubf[:, csl]), rr_s(wuq[sl]), s_su[sl],
                                           waits=[t_ul])
                else:
                    t_gl = c.dma("sync", rr_s(wgq[sl]), rr(gbf[:, csl]), s_wg[sl], waits=w_ld)
                    t_ul = c.dma("sync", rr_s(wuq[sl]), rr(ubf[:, csl]), s_wu[sl])
                for s in range(GSTRIPS):
                    ssl = nit % 2
                    w = [t_gl, t_lng,
                         t_silu_l.get(nit - 1) if nit >= 1 else t_prev_silu2]
                    for ci in range(NCH):
                        c.emit("tensor", lambda e, ci=ci, s=s, sl=sl: e.matmul(
                            ps_a[:], wgq[sl][:, ci * 128:(ci + 1) * 128],
                            lnTg[:, (s * NCH + ci) * 512:(s * NCH + ci) * 512 + 512],
                            start=(ci == 0), stop=(ci == NCH - 1)),
                            waits=w if ci == 0 else [], inc=(s_pe if ci == NCH - 1 else None))
                    t_gmm = c.tok(s_pe)
                    t_silu_l[nit] = c.emit(
                        "scalar", lambda e, ssl=ssl: e.activation(sgb[ssl][:], ps_a[:], AF.Silu),
                        waits=[t_gmm,
                               t_mult_l.get(nit - 2) if nit >= 2 else t_prev_mult2],
                        inc=s_act)
                    w = [t_ul, t_mult_l.get(nit - 1) if nit >= 1 else t_prev_mult2]
                    for ci in range(NCH):
                        c.emit("tensor", lambda e, ci=ci, s=s, sl=sl: e.matmul(
                            ps_b[:], wuq[sl][:, ci * 128:(ci + 1) * 128],
                            lnTg[:, (s * NCH + ci) * 512:(s * NCH + ci) * 512 + 512],
                            start=(ci == 0), stop=(ci == NCH - 1)),
                            waits=w if ci == 0 else [], inc=(s_pe if ci == NCH - 1 else None))
                    t_umm = c.tok(s_pe)
                    t_mult_l[nit] = c.emit(
                        "vector", lambda e, ic=ic, s=s, ssl=ssl: e.tensor_tensor(
                            ae[:, ic * GT + s * 512:ic * GT + s * 512 + 512],
                            sgb[ssl][:], ps_b[:], ALU.mult),
                        waits=[t_umm, t_silu_l[nit]], inc=s_dve)
                    if s == GSTRIPS - 1:
                        t_udone[ic] = t_mult_l[nit]
                        t_slot_used[sl] = t_mult_l[nit]
                    nit += 1
            t_prev_u2 = t_udone[ICH - 1]
            t_prev_silu2 = t_silu_l[nit - 1]
            t_prev_mult2 = t_mult_l[nit - 1]
            t_ae_done = t_udone[ICH - 1]
            t_dstop = {}
            t_ya = {}
            for dd in range(4):
                sl = dd % 2
                w_ld = [t_dstop.get(dd - 2) if dd >= 2 else t_prev_d2,
                        t_dspill[sl]]
                if (g, ei) == (0, 0) and dd < 2:
                    w_ld += [t_comb_done, t_l2w_all]
                if g == 1 and ei == 0 and dd < 2:
                    w_ld += t_spills_g0
                dsl = slice(dd * 512, (dd + 1) * 512)
                rrd = lambda ap: ap.rearrange("(c p) d -> p c d", p=128)
                rrd_s = lambda t: t[:].rearrange("p (c d) -> p c d", c=ICH)
                if g == 0:
                    t_dl = c.dma("gpsimd", rrd_s(wdq[sl]), rrd(dsrc[:, dsl]), s_wdc[sl], waits=w_ld)
                    t_dspill[sl] = c.dma("scalar", rrd(dbf[:, dsl]), rrd_s(wdq[sl]), s_sd[sl],
                                         waits=[t_dl])
                else:
                    t_dl = c.dma("sync", rrd_s(wdq[sl]), rrd(dbf[:, dsl]), s_wd[sl], waits=w_ld)
                for sub in range(GSUB):
                    k = dd * GSUB + sub
                    bank = ps_c if k % 2 == 0 else ps_d
                    if k >= 2:
                        w = [t_dl, t_ae_done, t_ya[k - 2]]
                    elif ei > 0:
                        w = [t_dl, t_ae_done, t_prev_ya_last]
                    else:
                        w = [t_dl, t_ae_done, t_fin_done]
                    for ic in range(ICH):
                        c.emit("tensor", lambda e, b=bank, ic=ic, sub=sub, sl=sl: e.matmul(
                            b[:], ae[:, ic * GT + sub * 128:ic * GT + sub * 128 + 128],
                            wdq[sl][:, ic * 512:(ic + 1) * 512],
                            start=(ic == 0), stop=(ic == ICH - 1)),
                            waits=w if ic == 0 else [], inc=(s_pe if ic == ICH - 1 else None))
                    t_dstop[dd] = c.tok(s_pe)
                    ysl = yg[:, sub * D + dd * 512:sub * D + dd * 512 + 512]
                    if ei == 0:
                        t_ya[k] = c.emit(
                            "vector", lambda e, b=bank, ysl=ysl: e.tensor_copy(ysl, b[:]),
                            waits=[t_dstop[dd]], inc=s_dve)
                    elif ce is None:
                        t_ya[k] = c.emit(
                            "vector", lambda e, b=bank, ysl=ysl: e.scalar_tensor_tensor(
                                ysl, b[:], 1.0, ysl, ALU.mult, ALU.add),
                            waits=[t_dstop[dd]], inc=s_dve)
                    else:
                        ccol = comb[:, (g * GSUB + sub) * 16 + ce:(g * GSUB + sub) * 16 + ce + 1]
                        t_ya[k] = c.emit(
                            "vector", lambda e, b=bank, ysl=ysl, ccol=ccol: e.scalar_tensor_tensor(
                                ysl, b[:], ccol, ysl, ALU.mult, ALU.add),
                            waits=[t_dstop[dd]], inc=s_dve)
            t_prev_d2 = t_dstop[3]
            t_prev_ya_last = t_ya[4 * GSUB - 1]
            if g == 0 and ei == len(eorder) - 1:
                t_spills_g0 = [c.tok(s) for s in (s_sg[0], s_sg[1], s_sg[2],
                                                  s_su[0], s_su[1], s_su[2],
                                                  s_sd[0], s_sd[1])]
        t_gu_pe_done = t_dstop[3]
        t_y_done = t_ya[4 * GSUB - 1]

        for sub in range(GSUB):
            sl = 0
            ti = g * GSUB + sub
            t_xfl = c.dma("sync", xf[sl][:], x2s_d[ti], s_xf[sl],
                          waits=[(s_x2w, 16 * NTILE), t_y_done, t_fin_roll[sl]])
            ysl = yg[:, sub * D:(sub + 1) * D]
            t_r = ln_stats(ysl, [t_y_done], dump3)
            t_n = c.emit("vector", lambda e, ysl=ysl: e.tensor_scalar(
                ysl, ysl, stats[:, 2:3], stats[:, 5:6], ALU.subtract, ALU.mult),
                waits=[t_r], inc=s_dvs)
            t_fin_roll[sl] = c.emit("vector", lambda e, ysl=ysl, sl=sl: e.tensor_tensor(
                outf[:], ysl, xf[sl][:], ALU.add),
                waits=[t_n, t_xfl, t_out_roll], inc=s_dve)
            t_out_roll = c.dma("scalar", out_d[ti], outf[:], s_ow,
                               waits=[t_fin_roll[sl]])
            t_outw[n_out] = t_out_roll
            n_out += 1
        t_fin_done = t_fin_roll[0]

    final = [c.tok(s) for s in (s_ow, s_x2w, s_l2w, s_zw, s_sg[0], s_sg[1], s_sg[2],
                                 s_su[0], s_su[1], s_su[2], s_sd[0], s_sd[1], s_dbg)]
    c.wait("sync", final)

    with nc.Block() as block:
        @block.tensor
        def _(e):
            for f in c.q["tensor"]:
                f(e)

        @block.vector
        def _(e):
            for f in c.q["vector"]:
                f(e)

        @block.scalar
        def _(e):
            for f in c.q["scalar"]:
                f(e)

        @block.gpsimd
        def _(e):
            for f in c.q["gpsimd"]:
                f(e)

        @block.sync
        def _(e):
            for f in c.q["sync"]:
                f(e)

    return nc


def host_consts():
    masks = np.zeros((128, 4 * 512), np.float32)
    for j in range(4):
        u = np.arange(128)[:, None]
        v = np.arange(512)[None, :]
        masks[:, j * 512:(j + 1) * 512] = np.where(v >= u + 128 * j, 0.0, -1e9)
    return {
        "masks": masks.astype(ml_dtypes.bfloat16),
        "ident": np.eye(128, dtype=ml_dtypes.bfloat16),
        "onesb": np.ones((128, 1), ml_dtypes.bfloat16),
        "rowones": np.ones((1, 128), np.float32),
        "ccols": np.tile(np.array([[EPS, -SHIFT]], np.float32), (128, 1)),
    }


def kernel(L0_attn_x, L0_attn_W_K, L0_attn_W_Q, L0_attn_W_V, L0_attn_W_O,
           L0_ffn_gate_w, L0_ffn_w_gate, L0_ffn_w_up, L0_ffn_w_down,
           L0_ffn_sw_gate, L0_ffn_sw_up, L0_ffn_sw_down):
    xs = np.ascontiguousarray(np.asarray(L0_attn_x, np.float32))
    shared = {
        "wqT": np.ascontiguousarray(np.asarray(L0_attn_W_Q, np.float32).transpose(0, 2, 1)),
        "wkT": np.ascontiguousarray(np.asarray(L0_attn_W_K, np.float32).transpose(0, 2, 1)),
        "wvT": np.ascontiguousarray(np.asarray(L0_attn_W_V, np.float32).transpose(0, 2, 1)),
        "woT": np.ascontiguousarray(np.asarray(L0_attn_W_O, np.float32).T),
        "gwT": np.ascontiguousarray(np.asarray(L0_ffn_gate_w, np.float32).T),
        "wg": np.ascontiguousarray(np.asarray(L0_ffn_w_gate, np.float32)),
        "wu": np.ascontiguousarray(np.asarray(L0_ffn_w_up, np.float32)),
        "wd": np.ascontiguousarray(np.asarray(L0_ffn_w_down, np.float32)),
        "swg": np.ascontiguousarray(np.asarray(L0_ffn_sw_gate, np.float32)),
        "swu": np.ascontiguousarray(np.asarray(L0_ffn_sw_up, np.float32)),
        "swd": np.ascontiguousarray(np.asarray(L0_ffn_sw_down, np.float32)),
        **host_consts(),
    }
    nc = build()
    in_maps = [{"x": xs[b], **shared} for b in range(8)]
    res = run_bass_kernel_spmd(nc, in_maps, core_ids=list(range(8)))
    out = np.stack([np.asarray(res.results[b]["out"], np.float32).reshape(S, D)
                    for b in range(8)])
    return out


# version 03
# speedup vs baseline: 1.0450x; 1.0450x over previous
import math
import numpy as np
import ml_dtypes
import concourse.bass as bass
import concourse.mybir as mybir
from concourse.bass_utils import run_bass_kernel_spmd

F32 = mybir.dt.float32
BF16 = mybir.dt.bfloat16
AF = mybir.ActivationFunctionType
ALU = mybir.AluOpType
AX = mybir.AxisListType

S, D, H, DH = 4096, 2048, 16, 128
NCH = D // 128
NTILE = S // 128
NSTRIP = S // 512
I_MOE = 1408
ICH = I_MOE // 128
NG = 4
GT = S // NG
GSTRIPS = GT // 512
GSUB = GT // 128
SCALE = 1.0 / math.sqrt(128.0)
SHIFT = 25.0
EPS = 1e-5
BIS_ITERS = 26

ENGS = ("tensor", "vector", "scalar", "gpsimd", "sync")


class Ctx:
    def __init__(self, nc):
        self.nc = nc
        self.q = {e: [] for e in ENGS}
        self.cnt = {}
        self.sems = {}
        self.lastw = {}

    def sem(self, name):
        h = self.nc.alloc_semaphore(name=name)
        self.sems[name] = h
        self.cnt[name] = 0
        return name

    def emit(self, eng, fn, waits=(), inc=None, amt=1):
        nw = []
        for t in waits:
            if t is None:
                continue
            s, v = t
            assert v <= self.cnt[s], f"forward wait {s} {v} > {self.cnt[s]}"
            if self.lastw.get((eng, s), -1) < v:
                nw.append((self.sems[s], v))
                self.lastw[(eng, s)] = v
        tok = None
        semh = None
        if inc is not None:
            self.cnt[inc] += amt
            tok = (inc, self.cnt[inc])
            semh = self.sems[inc]

        def play(e, nw=nw, fn=fn, semh=semh, amt=amt):
            for (hh, v) in nw:
                e.wait_ge(hh, v)
            ins = fn(e)
            if semh is not None:
                ins.then_inc(semh, amt)
        self.q[eng].append(play)
        return tok

    def wait(self, eng, waits):
        self.emit(eng, lambda e: e.nop(), waits=waits)

    def dma(self, eng, out, in_, sem, waits=()):
        waits = list(waits) + [(sem, self.cnt[sem])]
        return self.emit(eng, lambda e, o=out, i=in_: e.dma_start(out=o, in_=i),
                         waits=waits, inc=sem, amt=16)

    def tok(self, sem):
        return (sem, self.cnt[sem])


def build(debug=False):
    nc = bass.Bass("TRN2", target_bir_lowering=False)

    x_d = nc.dram_tensor("x", [S, D], F32, kind="ExternalInput").ap()
    wqT_d = nc.dram_tensor("wqT", [H, D, DH], F32, kind="ExternalInput").ap()
    wkT_d = nc.dram_tensor("wkT", [H, D, DH], F32, kind="ExternalInput").ap()
    wvT_d = nc.dram_tensor("wvT", [H, D, DH], F32, kind="ExternalInput").ap()
    woT_d = nc.dram_tensor("woT", [D, D], F32, kind="ExternalInput").ap()
    gwT_d = nc.dram_tensor("gwT", [D, 16], F32, kind="ExternalInput").ap()
    wg_d = nc.dram_tensor("wg", [16, D, I_MOE], F32, kind="ExternalInput").ap()
    wu_d = nc.dram_tensor("wu", [16, D, I_MOE], F32, kind="ExternalInput").ap()
    wd_d = nc.dram_tensor("wd", [16, I_MOE, D], F32, kind="ExternalInput").ap()
    swg_d = nc.dram_tensor("swg", [D, 2 * I_MOE], F32, kind="ExternalInput").ap()
    swu_d = nc.dram_tensor("swu", [D, 2 * I_MOE], F32, kind="ExternalInput").ap()
    swd_d = nc.dram_tensor("swd", [2 * I_MOE, D], F32, kind="ExternalInput").ap()
    masks_d = nc.dram_tensor("masks", [128, 4 * 512], BF16, kind="ExternalInput").ap()
    ident_d = nc.dram_tensor("ident", [128, 128], BF16, kind="ExternalInput").ap()
    onesb_d = nc.dram_tensor("onesb", [128, 1], BF16, kind="ExternalInput").ap()
    rowones_d = nc.dram_tensor("rowones", [1, 128], F32, kind="ExternalInput").ap()
    ccols_d = nc.dram_tensor("ccols", [128, 2], F32, kind="ExternalInput").ap()

    out_d = nc.dram_tensor("out", [NTILE, 128, D], F32, kind="ExternalOutput").ap()

    x2s_d = nc.dram_tensor("x2s", [NTILE, 128, D], F32, kind="Internal").ap()
    ln2Ts_d = nc.dram_tensor("ln2Ts", [NSTRIP, NCH, 128, 512], BF16, kind="Internal").ap()
    zs_d = nc.dram_tensor("zs", [H, 128, S], BF16, kind="Internal").ap()
    wgb_d = nc.dram_tensor("wgb", [16, ICH, 128, NCH * 128], BF16, kind="Internal").ap()
    wub_d = nc.dram_tensor("wub", [16, ICH, 128, NCH * 128], BF16, kind="Internal").ap()
    wdb_d = nc.dram_tensor("wdb", [16, 4, 128, ICH * 512], BF16, kind="Internal").ap()
    swgb_d = nc.dram_tensor("swgb", [2, ICH, 128, NCH * 128], BF16, kind="Internal").ap()
    swub_d = nc.dram_tensor("swub", [2, ICH, 128, NCH * 128], BF16, kind="Internal").ap()
    swdb_d = nc.dram_tensor("swdb", [2, 4, 128, ICH * 512], BF16, kind="Internal").ap()

    dbg = {}
    if debug:
        dbg["ln1T"] = nc.dram_tensor("dbg_ln1T", [128, NCH * S], BF16, kind="ExternalOutput").ap()
        dbg["z"] = nc.dram_tensor("dbg_z", [H, 128, S], BF16, kind="ExternalOutput").ap()
        dbg["x2"] = nc.dram_tensor("dbg_x2", [NTILE, 128, D], F32, kind="ExternalOutput").ap()
        dbg["comb"] = nc.dram_tensor("dbg_comb", [128, NTILE * 16], F32, kind="ExternalOutput").ap()
        dbg["ln2T"] = nc.dram_tensor("dbg_ln2T", [NSTRIP, NCH, 128, 512], BF16, kind="ExternalOutput").ap()

    A = nc.alloc_sbuf_tensor
    ident = A("c_ident", [128, 128], BF16)
    masks = A("c_masks", [128, 4 * 512], BF16)
    onesb = A("c_onesb", [128, 1], BF16)
    rowones = A("c_rowones", [1, 128], F32)
    ccols = A("c_ccols", [128, 2], F32)
    gwT_sb = A("gwT_sb", [128, 16 * 16], BF16)
    comb = A("comb", [128, NTILE * 16], F32)
    probs = A("probs", [128, NTILE * 16], F32)
    stats = A("stats", [128, 16], F32)
    rsum = A("rsum", [1, 512], F32)
    eps_c = ccols[:, 0:1]
    shm_c = ccols[:, 1:2]

    A_BYTES = 131072
    B_BYTES = 68000
    top = nc.sbuf_top
    base_A = top - A_BYTES
    base_B = base_A - B_BYTES
    assert nc.sbuf_base + 64 < base_B, (nc.sbuf_base, base_B)

    def at(name, shape, dtype, off):
        nbytes = shape[1] * (2 if dtype == BF16 else 4)
        assert off + nbytes <= top, (name, off, nbytes)
        return nc.alloc_sbuf_tensor_at(name, shape, dtype, offset=off)

    ln1T = at("ln1T", [128, NCH * S], BF16, base_A)
    woT_sb = at("woT_sb", [128, NCH * D], BF16, base_A)
    zsb = [at(f"zsb{i}", [128, H * 512], BF16, base_A + 65536 + i * 16384) for i in range(2)]
    ao = at("ao", [128, D], F32, base_A + 98304)
    x2st = at("x2st", [128, D], F32, base_A + 106496)
    yg = at("yg", [128, GSUB * D], F32, base_A)
    lnTg = at("lnTg", [128, GSTRIPS * NCH * 512], BF16, base_A + 65536)
    ae = at("ae", [128, ICH * GT], BF16, base_A + 98304)
    dump3 = at("dump3", [128, D], F32, base_A + 120832)

    xt = [at(f"xt{i}", [128, D], F32, base_B + i * 8192) for i in range(2)]
    dump0 = at("dump0", [128, D], F32, base_B + 32768)
    dump2 = at("dump2", [128, D], F32, base_B + 43008)
    ln1b = at("ln1b", [128, D], BF16, base_B + 24576)
    wqkv = [at(f"wqkv{i}", [128, 3 * NCH * DH], BF16, base_B + i * 12288) for i in range(2)]
    Qf = at("Qf", [128, S], BF16, base_B + 24576)
    Kf = at("Kf", [128, S], BF16, base_B + 32768)
    Vt = at("Vt", [128, S], BF16, base_B + 40960)
    vtmp = at("vtmp", [128, 512], BF16, base_B + 49152)
    zh = at("zh", [128, S], BF16, base_B + 50176)
    atb = [at(f"atb{i}", [128, 512], BF16, base_B + 58368 + i * 1024) for i in range(3)]
    bc_sb = at("bc_sb", [128, 512], F32, base_B + 61440)
    xin = [at(f"xin{i}", [128, D], F32, base_B + i * 8192) for i in range(2)]
    ln2b = at("ln2b", [128, D], BF16, base_B + 16384)
    l2st = at("l2st", [128, NCH * 512], BF16, base_B + 20480)
    ptmp = at("ptmp", [128, 16], F32, base_B + 36864)
    gst = at("gst", [128, 7 * NTILE], F32, base_B)
    cmpa = at("cmpa", [128, NTILE * 16], F32, base_B + 896)
    wgq = [at(f"wgq{i}", [128, NCH * 128], BF16, base_B + i * 4096) for i in range(3)]
    wuq = [at(f"wuq{i}", [128, NCH * 128], BF16, base_B + 12288 + i * 4096) for i in range(3)]
    wdq = [at(f"wdq{i}", [128, ICH * 512], BF16, base_B + 24576 + i * 11264) for i in range(2)]
    sgb = [at(f"sgb{i}", [128, 512], F32, base_B + 47104 + i * 2048) for i in range(2)]
    xf = [at(f"xf{i}", [128, D], F32, base_B + 51200) for i in range(1)]
    outf = at("outf", [128, D], F32, base_B + 59392)

    ps_a = nc.alloc_psum_tensor("ps_a", [128, 512], F32)
    ps_b = nc.alloc_psum_tensor("ps_b", [128, 512], F32)
    ps_c = nc.alloc_psum_tensor("ps_c", [128, 512], F32)
    ps_d = nc.alloc_psum_tensor("ps_d", [128, 512], F32)
    ps_sum = nc.alloc_psum_tensor("ps_sum", [1, 512], F32)
    ps_bc = nc.alloc_psum_tensor("ps_bc", [128, 512], F32)
    ps_t1 = nc.alloc_psum_tensor("ps_t1", [128, 1024], BF16)
    ps_t2 = nc.alloc_psum_tensor("ps_t2", [128, 1024], BF16)

    c = Ctx(nc)
    s_pe = c.sem("s_pe")
    s_act = c.sem("s_act")
    s_dve = c.sem("s_dve")
    s_dvs = c.sem("s_dvs")
    s_acs = c.sem("s_acs")
    s_ci = c.sem("s_ci")
    s_gw = c.sem("s_gw")
    s_x = [c.sem("s_x0"), c.sem("s_x1")]
    s_wq = [c.sem("s_wq0"), c.sem("s_wq1")]
    s_zw = c.sem("s_zw")
    s_zs = [c.sem("s_zs0"), c.sem("s_zs1")]
    s_x2w = c.sem("s_x2w")
    s_l2w = c.sem("s_l2w")
    s_wo = c.sem("s_wo")
    s_lng = c.sem("s_lng")
    s_wg = [c.sem("s_wg0"), c.sem("s_wg1"), c.sem("s_wg2")]
    s_wu = [c.sem("s_wu0"), c.sem("s_wu1"), c.sem("s_wu2")]
    s_wd = [c.sem("s_wd0"), c.sem("s_wd1")]
    s_wgc = [c.sem("s_wgc0"), c.sem("s_wgc1"), c.sem("s_wgc2")]
    s_wuc = [c.sem("s_wuc0"), c.sem("s_wuc1"), c.sem("s_wuc2")]
    s_wdc = [c.sem("s_wdc0"), c.sem("s_wdc1")]
    s_sg = [c.sem("s_sg0"), c.sem("s_sg1"), c.sem("s_sg2")]
    s_su = [c.sem("s_su0"), c.sem("s_su1"), c.sem("s_su2")]
    s_sd = [c.sem("s_sd0"), c.sem("s_sd1")]
    s_xf = [c.sem("s_xf0"), c.sem("s_xf1")]
    s_ow = c.sem("s_ow")
    s_dbg = c.sem("s_dbg")

    ln1T_r = ln1T[:].rearrange("p (c t) -> p c t", c=NCH)

    c.dma("sync", ident[:], ident_d, s_ci)
    c.dma("sync", masks[:], masks_d, s_ci)
    c.dma("sync", onesb[:], onesb_d, s_ci)
    c.dma("sync", rowones[:], rowones_d, s_ci)
    c.dma("sync", ccols[:], ccols_d, s_ci)
    c.dma("gpsimd", gwT_sb[:].rearrange("p (c e) -> p c e", c=NCH),
          gwT_d.rearrange("(c p) e -> p c e", p=128), s_gw)
    t_init_in = c.tok(s_ci)
    t_init_cst = c.tok(s_gw)

    def ln_stats(src_ap, waits, dmp, cb=0):
        sc = lambda k: stats[:, cb + k:cb + k + 1]
        t1 = c.emit("scalar", lambda e, a=src_ap, d=dmp: e.activation(
            d[:], a, AF.Identity, accum_out=sc(0)), waits=waits, inc=s_act)
        t2 = c.emit("scalar", lambda e, a=src_ap, d=dmp: e.activation(
            d[:], a, AF.Square, accum_out=sc(1)), waits=[t1], inc=s_act)
        c.emit("vector", lambda e: e.tensor_scalar_mul(sc(2), sc(0), 1.0 / D),
               waits=[t2], inc=s_dve)
        t3 = c.emit("vector", lambda e: e.tensor_scalar_mul(sc(3), sc(1), 1.0 / D),
                    inc=s_dvs)
        t4 = c.emit("vector", lambda e: e.tensor_tensor(sc(4), sc(2), sc(2), ALU.mult),
                    waits=[t3], inc=s_dvs)
        t5 = c.emit("vector", lambda e: e.tensor_tensor(sc(3), sc(3), sc(4), ALU.subtract),
                    waits=[t4], inc=s_dve)
        t6 = c.emit("scalar", lambda e: e.activation(sc(5), sc(3), AF.Ln, bias=eps_c),
                    waits=[t5], inc=s_acs)
        t7 = c.emit("scalar", lambda e: e.activation(sc(5), sc(5), AF.Exp, scale=-0.5),
                    waits=[t6], inc=s_act)
        return t7

    t_apply = {}
    t_evac = {}
    t_trdone = {}
    t_xload = [None, None]
    for i in range(NTILE):
        sl = i % 2
        w = [t_apply.get(i - 2)]
        t_xload[sl] = c.dma("sync", xt[sl][:], x_d[i * 128:(i + 1) * 128, :], s_x[sl], waits=w)
        t_r = ln_stats(xt[sl][:], [t_xload[sl], t_init_in], dump0)
        w = [t_r]
        if i >= 1:
            w.append(t_trdone[2 * i - 1])
        t_apply[i] = c.emit("vector", lambda e, sl=sl: e.tensor_scalar(
            ln1b[:], xt[sl][:], stats[:, 2:3], stats[:, 5:6], ALU.subtract, ALU.mult),
            waits=w, inc=s_dve)
        for half in range(2):
            bank = ps_t1 if (i * 2 + half) % 2 == 0 else ps_t2
            w = [t_apply[i], t_evac.get((i * 2 + half) - 2)]
            for k in range(8):
                ch = half * 8 + k
                c.emit("tensor", lambda e, b=bank, k=k, ch=ch: e.transpose(
                    b[:, k * 128:(k + 1) * 128], ln1b[:, ch * 128:(ch + 1) * 128], ident[:]),
                    waits=w if k == 0 else [], inc=(s_pe if k == 7 else None))
            t_trdone[i * 2 + half] = c.tok(s_pe)
            dst = ln1T_r[:, half * 8:(half + 1) * 8, i * 128:(i + 1) * 128]
            t_evac[i * 2 + half] = c.emit(
                "vector", lambda e, b=bank, d=dst: e.tensor_copy(d, b[:]),
                waits=[t_trdone[i * 2 + half]], inc=s_dve)
    t_ln1T_done = c.tok(s_dve)

    if debug:
        c.dma("scalar", dbg["ln1T"], ln1T[:], s_dbg, waits=[t_ln1T_done])

    t_qkv_used = {}
    t_zdma = {}
    t_zmul = {}
    t_recip = {}
    t_bccopy = {}

    for h in range(H):
        sl = h % 2
        w = [t_ln1T_done if h == 0 else None, t_qkv_used.get(h - 2)]
        buf = wqkv[sl]
        t_w = None
        for mi, wsrc in enumerate((wqT_d, wkT_d, wvT_d)):
            src = wsrc[h].rearrange("(c p) m -> p c m", p=128)
            dst = buf[:, mi * 2048:(mi + 1) * 2048].rearrange("p (c m) -> p c m", c=NCH)
            t_w = c.dma("gpsimd", dst, src, s_wq[sl],
                        waits=w if mi == 0 else [])
        t_vt_done = None
        t_qkv_evac = {}
        for mi in range(3):
            dstt = (Qf, Kf, None)[mi]
            for s in range(NSTRIP):
                bank = ps_a if s % 2 == 0 else ps_b
                w = [t_w]
                if h > 0 and mi == 0 and s == 0:
                    w.append(t_zmul[(h - 1, NSTRIP - 1)])
                for ci in range(NCH):
                    c.emit("tensor", lambda e, b=bank, buf=buf, mi=mi, ci=ci, s=s: e.matmul(
                        b[:], buf[:, mi * 2048 + ci * 128:mi * 2048 + (ci + 1) * 128],
                        ln1T[:, ci * S + s * 512:ci * S + (s + 1) * 512],
                        start=(ci == 0), stop=(ci == NCH - 1)),
                        waits=w if ci == 0 else [],
                        inc=(s_pe if ci == NCH - 1 else None))
                t_mm = c.tok(s_pe)
                if mi < 2:
                    t_qkv_evac[(mi, s)] = c.emit(
                        "scalar", lambda e, d=dstt, b=bank, s=s: e.copy(
                            d[:, s * 512:(s + 1) * 512], b[:]),
                        waits=[t_mm], inc=s_act)
                else:
                    t_ev = c.emit("scalar", lambda e, b=bank: e.copy(vtmp[:], b[:]),
                                  waits=[t_mm], inc=s_act)
                    tb = ps_t1 if s % 2 == 0 else ps_t2
                    for k in range(4):
                        c.emit("tensor", lambda e, tb=tb, k=k: e.transpose(
                            tb[:, k * 128:(k + 1) * 128], vtmp[:, k * 128:(k + 1) * 128], ident[:]),
                            waits=[t_ev] if k == 0 else [], inc=(s_pe if k == 3 else None))
                    t_tr = c.tok(s_pe)
                    t_vt_done = c.emit("vector", lambda e, tb=tb, s=s: e.tensor_copy(
                        Vt[:, s * 512:(s + 1) * 512], tb[:, 0:512]),
                        waits=[t_tr], inc=s_dve)
        t_qkv_used[h] = c.tok(s_pe)
        t_k_done = t_qkv_evac[(1, NSTRIP - 1)]
        t_q_done = t_qkv_evac[(0, NSTRIP - 1)]

        for qs in range(NSTRIP):
            npc = 4 * qs + 4
            zbank = ps_c if qs % 2 == 0 else ps_d
            zoth = ps_d if qs % 2 == 0 else ps_c
            sring = [ps_a, ps_b, zoth]
            t_exp = {}
            t_scores = {}
            last_avstop = {}

            def emit_score(pc, qs=qs, npc=npc, sring=sring):
                bank = sring[pc % 3]
                w = [t_k_done, t_q_done, t_exp.get(pc - 3)]
                if qs == 0 and pc <= 1:
                    w.append(t_vt_done)
                if pc % 3 == 2 and pc == 2:
                    pk = (h, qs - 1) if qs > 0 else (h - 1, NSTRIP - 1)
                    if pk in t_zmul:
                        w.append(t_zmul[pk])
                t_scores[pc] = c.emit("tensor", lambda e, b=bank, pc=pc, qs=qs: e.matmul(
                    b[:], Kf[:, pc * 128:(pc + 1) * 128], Qf[:, qs * 512:(qs + 1) * 512],
                    start=True, stop=True), waits=w, inc=s_pe)

            emit_score(0)
            if npc > 1:
                emit_score(1)
            for pc in range(npc):
                bank = sring[pc % 3]
                wexp = [t_scores[pc], last_avstop.get(pc - 3)]
                if pc >= 4 * qs:
                    j = pc - 4 * qs
                    t_msk = c.emit("vector", lambda e, b=bank, j=j: e.tensor_tensor(
                        b[:], b[:], masks[:, j * 512:(j + 1) * 512], ALU.add),
                        waits=[t_scores[pc]], inc=s_dve)
                    wexp[0] = t_msk
                ab = atb[pc % 3]
                t_exp[pc] = c.emit("scalar", lambda e, ab=ab, b=bank: e.activation(
                    ab[:], b[:], AF.Exp, bias=shm_c, scale=SCALE),
                    waits=wexp, inc=s_act)
                if pc + 2 < npc:
                    emit_score(pc + 2)
                w_s = [t_exp[pc]]
                if pc == 0 and (h > 0 or qs > 0):
                    w_s.append(t_recip[(h, qs - 1) if qs > 0 else (h - 1, NSTRIP - 1)])
                c.emit("tensor", lambda e, ab=ab, pc=pc, npc=npc: e.matmul(
                    ps_sum[:], onesb[:], ab[:], start=(pc == 0), stop=(pc == npc - 1)),
                    waits=w_s, inc=(s_pe if pc == npc - 1 else None))
                w_z = [t_vt_done]
                if pc == 0 and (h * NSTRIP + qs) >= 2:
                    pqs = qs - 2 if qs >= 2 else qs + NSTRIP - 2
                    ph = h if qs >= 2 else h - 1
                    w_z.append(t_zmul[(ph, pqs)])
                last_avstop[pc] = c.emit("tensor", lambda e, ab=ab, zb=zbank, pc=pc, npc=npc:
                                         e.matmul(zb[:], Vt[:, pc * 128:(pc + 1) * 128], ab[:],
                                                  start=(pc == 0), stop=(pc == npc - 1)),
                                         waits=w_z, inc=s_pe)
            t_sum_stop = (s_pe, c.cnt[s_pe] - 1)
            t_av_stop = last_avstop[npc - 1]
            t_recip[(h, qs)] = c.emit("vector", lambda e: e.reciprocal(rsum[:], ps_sum[:]),
                                      waits=[t_sum_stop], inc=s_dve)
            w_bc = [t_recip[(h, qs)], t_init_in]
            if (h, qs) != (0, 0):
                pk = (h, qs - 1) if qs > 0 else (h - 1, NSTRIP - 1)
                w_bc += [t_bccopy[pk], t_zmul[pk]]
            t_bcmm = c.emit("tensor", lambda e: e.matmul(ps_bc[:], rowones[:], rsum[:],
                                                         start=True, stop=True),
                            waits=w_bc, inc=s_pe)
            t_bccopy[(h, qs)] = c.emit("scalar", lambda e: e.copy(bc_sb[:], ps_bc[:]),
                                       waits=[t_bcmm], inc=s_act)
            w_zm = [t_bccopy[(h, qs)], t_av_stop]
            if h > 0 and qs == 0:
                w_zm.append(t_zdma[h - 1])
            t_zmul[(h, qs)] = c.emit(
                "vector", lambda e, qs=qs, zb=zbank:
                e.tensor_tensor(zh[:, qs * 512:(qs + 1) * 512], zb[:], bc_sb[:], ALU.mult),
                waits=w_zm, inc=s_dve)
        t_zdma[h] = c.dma("scalar", zs_d[h], zh[:], s_zw, waits=[t_zmul[(h, NSTRIP - 1)]])

    t_p1_pe_done = c.tok(s_pe)
    t_p1_dve_done = c.tok(s_dve)
    t_zdma_all = c.tok(s_zw)

    if debug:
        c.dma("scalar", dbg["z"], zs_d, s_dbg, waits=[t_zdma_all])

    t_wo = c.dma("gpsimd", woT_sb[:].rearrange("p (c d) -> p c d", c=NCH),
                 woT_d.rearrange("(c p) d -> p c d", p=128),
                 s_wo, waits=[t_p1_pe_done])

    t_zs_used = {}
    t_zsl = [None, None]
    t_x2add = {}
    t_x2w_l = {}
    t_gexp = {}
    t_probs = {}
    t_l2w = {}
    t_l2ev = {}
    t_l2b_prev = [None]
    t_tr2done_l = [None]
    zs_r = zs_d.rearrange("h p (s f) -> s p h f", s=NSTRIP)

    deferred = [None]

    def emit_deferred():
        if deferred[0] is None:
            return
        deferred[0]()
        deferred[0] = None

    for t in range(NTILE):
        st = t // 4
        sl = st % 2
        sl2 = t % 2
        if t % 4 == 0:
            w = [t_zdma_all, t_p1_dve_done, t_zs_used.get(st - 2)]
            t_zsl[sl] = c.dma("sync", zsb[sl][:].rearrange("p (h f) -> p h f", h=H),
                              zs_r[st], s_zs[sl], waits=w)
        t_xl = c.dma("sync", xin[sl2][:], x_d[t * 128:(t + 1) * 128, :], s_x[sl2],
                     waits=[t_p1_dve_done, t_x2add.get(t - 2)])
        t_wod = {}
        for dd in range(4):
            bank = ps_a if dd % 2 == 0 else ps_b
            w = [t_wo, t_zsl[sl], t_x2add.get(t - 1)]
            for ci in range(NCH):
                c.emit("tensor", lambda e, b=bank, ci=ci, dd=dd, t=t, sl=sl: e.matmul(
                    b[:], zsb[sl][:, ci * 512 + (t % 4) * 128:ci * 512 + (t % 4) * 128 + 128],
                    woT_sb[:, ci * D + dd * 512:ci * D + dd * 512 + 512],
                    start=(ci == 0), stop=(ci == NCH - 1)),
                    waits=w if ci == 0 else [], inc=(s_pe if ci == NCH - 1 else None))
            t_mm = c.tok(s_pe)
            t_wod[dd] = c.emit("scalar", lambda e, b=bank, dd=dd: e.copy(
                ao[:, dd * 512:(dd + 1) * 512], b[:]), waits=[t_mm], inc=s_act)
            if dd == 1:
                emit_deferred()
        t_zs_used[st] = c.tok(s_pe)
        t_r = ln_stats(ao[:], [t_wod[3]], dump2)
        t_norm = c.emit("vector", lambda e: e.tensor_scalar(
            ao[:], ao[:], stats[:, 2:3], stats[:, 5:6], ALU.subtract, ALU.mult),
            waits=[t_r], inc=s_dvs)
        t_x2add[t] = c.emit("vector", lambda e, sl2=sl2: e.tensor_tensor(
            x2st[:], ao[:], xin[sl2][:], ALU.add),
            waits=[t_norm, t_xl, t_x2w_l.get(t - 1), t_l2b_prev[0]], inc=s_dve)
        t_x2w_l[t] = c.dma("scalar", x2s_d[t], x2st[:], s_x2w, waits=[t_x2add[t]])
        t_r2 = ln_stats(x2st[:], [t_x2add[t], t_l2b_prev[0]], dump2, cb=8)

        def make_deferred(t=t, st=st, t_r2=t_r2):
            def go():
                t_l2b = c.emit("vector", lambda e: e.tensor_scalar(
                    ln2b[:], x2st[:], stats[:, 10:11], stats[:, 13:14],
                    ALU.subtract, ALU.mult),
                    waits=[t_r2, t_tr2done_l[0]], inc=s_dve)
                t_l2b_prev[0] = t_l2b
                for half in range(2):
                    bank = ps_t1 if half == 0 else ps_t2
                    w = [t_l2b, t_l2ev.get(half)]
                    for k in range(8):
                        ch = half * 8 + k
                        c.emit("tensor", lambda e, b=bank, k=k, ch=ch: e.transpose(
                            b[:, k * 128:(k + 1) * 128], ln2b[:, ch * 128:(ch + 1) * 128],
                            ident[:]),
                            waits=w if k == 0 else [], inc=(s_pe if k == 7 else None))
                    t_trh = c.tok(s_pe)
                    if half == 1:
                        t_tr2done_l[0] = t_trh
                    dst = l2st[:].rearrange("p (c f) -> p c f", c=NCH)[
                        :, half * 8:(half + 1) * 8, (t % 4) * 128:(t % 4) * 128 + 128]
                    wv = [t_trh]
                    if st > 0 and t % 4 == 0:
                        wv.append(t_l2w[st - 1])
                    t_l2ev[half] = c.emit("vector", lambda e, b=bank, d=dst: e.tensor_copy(d, b[:]),
                                          waits=wv, inc=s_dve)
                w = [t_l2ev[0], t_l2ev[1], t_init_cst, t_gexp.get(t - 1)]
                for ci in range(NCH):
                    c.emit("tensor", lambda e, ci=ci, t=t: e.matmul(
                        ps_bc[:, 0:16],
                        l2st[:, ci * 512 + (t % 4) * 128:ci * 512 + (t % 4) * 128 + 128],
                        gwT_sb[:, ci * 16:(ci + 1) * 16],
                        start=(ci == 0), stop=(ci == NCH - 1)),
                        waits=w if ci == 0 else [], inc=(s_pe if ci == NCH - 1 else None))
                t_gmm = c.tok(s_pe)
                t_nm = c.emit("vector", lambda e: e.reduce_max(stats[:, 6:7], ps_bc[:, 0:16],
                                                               axis=AX.X, negate=True),
                              waits=[t_gmm], inc=s_dve)
                t_gexp[t] = c.emit("scalar", lambda e: e.activation(ptmp[:], ps_bc[:, 0:16],
                                                                    AF.Exp, bias=stats[:, 6:7]),
                                   waits=[t_nm], inc=s_act)
                t_rs = c.emit("vector", lambda e: e.reduce_sum(stats[:, 7:8], ptmp[:], axis=AX.X),
                              waits=[t_gexp[t]], inc=s_dvs)
                t_rw = c.emit("vector", lambda e: e.reciprocal(stats[:, 14:15], stats[:, 7:8]),
                              waits=[t_rs], inc=s_dvs)
                t_probs[t] = c.emit("vector", lambda e, t=t: e.tensor_scalar_mul(
                    probs[:, t * 16:(t + 1) * 16], ptmp[:], stats[:, 14:15]),
                    waits=[t_rw], inc=s_dve)
                if t % 4 == 3:
                    t_l2w[st] = c.dma("scalar", ln2Ts_d[st].rearrange("c p f -> p c f"),
                                      l2st[:].rearrange("p (c f) -> p c f", c=NCH),
                                      s_l2w, waits=[t_l2ev[0], t_l2ev[1]])
            return go
        deferred[0] = make_deferred()
    emit_deferred()

    LO, HI, CNT, PRD, MID, D1, D2 = range(7)
    col = lambda k: gst[:, k * NTILE:(k + 1) * NTILE]
    c.emit("vector", lambda e: e.memset(col(LO), 0.0),
           waits=[t_probs[NTILE - 1]], inc=s_dve)
    for t in range(NTILE):
        c.emit("vector", lambda e, t=t: e.reduce_max(
            gst[:, HI * NTILE + t:HI * NTILE + t + 1],
            probs[:, t * 16:(t + 1) * 16], axis=AX.X),
            inc=(s_dvs if t == NTILE - 1 else None))
    t_bis = c.tok(s_dvs)
    for it in range(BIS_ITERS):
        t_m1 = c.emit("vector", lambda e: e.tensor_tensor(col(MID), col(LO), col(HI), ALU.add),
                      waits=[t_bis], inc=s_dvs)
        t_m2 = c.emit("vector", lambda e: e.tensor_scalar_mul(col(MID), col(MID), 0.5),
                      waits=[t_m1], inc=s_dvs)
        for t in range(NTILE):
            c.emit("vector", lambda e, t=t: e.tensor_scalar(
                cmpa[:, t * 16:(t + 1) * 16], probs[:, t * 16:(t + 1) * 16],
                gst[:, MID * NTILE + t:MID * NTILE + t + 1], None, ALU.is_ge),
                waits=[t_m2] if t == 0 else [], inc=(s_dvs if t == NTILE - 1 else None))
        t_cmp = c.tok(s_dvs)
        for t in range(NTILE):
            c.emit("vector", lambda e, t=t: e.reduce_sum(
                gst[:, CNT * NTILE + t:CNT * NTILE + t + 1],
                cmpa[:, t * 16:(t + 1) * 16], axis=AX.X),
                waits=[t_cmp] if t == 0 else [], inc=(s_dvs if t == NTILE - 1 else None))
        t_cnt = c.tok(s_dvs)
        t_prd = c.emit("vector", lambda e: e.tensor_scalar(col(PRD), col(CNT), 4.0, None, ALU.is_ge),
                       waits=[t_cnt], inc=s_dvs)
        c.emit("vector", lambda e: e.tensor_tensor(col(D1), col(MID), col(LO), ALU.subtract),
               waits=[t_prd])
        t_d2 = c.emit("vector", lambda e: e.tensor_tensor(col(D2), col(HI), col(MID), ALU.subtract),
                      inc=s_dvs)
        c.emit("vector", lambda e: e.tensor_tensor(col(D1), col(D1), col(PRD), ALU.mult),
               waits=[t_d2])
        t_m3 = c.emit("vector", lambda e: e.tensor_tensor(col(D2), col(D2), col(PRD), ALU.mult),
                      inc=s_dvs)
        c.emit("vector", lambda e: e.tensor_tensor(col(LO), col(LO), col(D1), ALU.add),
               waits=[t_m3])
        t_bis = c.emit("vector", lambda e: e.tensor_tensor(col(HI), col(MID), col(D2), ALU.add),
                       inc=s_dvs)
    for t in range(NTILE):
        c.emit("vector", lambda e, t=t: e.tensor_scalar(
            cmpa[:, t * 16:(t + 1) * 16], probs[:, t * 16:(t + 1) * 16],
            gst[:, LO * NTILE + t:LO * NTILE + t + 1], None, ALU.is_ge),
            waits=[t_bis] if t == 0 else [], inc=(s_dvs if t == NTILE - 1 else None))
    t_c1 = c.tok(s_dvs)
    for t in range(NTILE):
        c.emit("vector", lambda e, t=t: e.tensor_tensor(
            cmpa[:, t * 16:(t + 1) * 16], cmpa[:, t * 16:(t + 1) * 16],
            probs[:, t * 16:(t + 1) * 16], ALU.mult),
            waits=[t_c1] if t == 0 else [], inc=(s_dvs if t == NTILE - 1 else None))
    t_c2 = c.tok(s_dvs)
    for t in range(NTILE):
        c.emit("vector", lambda e, t=t: e.reduce_sum(
            gst[:, CNT * NTILE + t:CNT * NTILE + t + 1],
            cmpa[:, t * 16:(t + 1) * 16], axis=AX.X),
            waits=[t_c2] if t == 0 else [], inc=(s_dvs if t == NTILE - 1 else None))
    t_c3 = c.tok(s_dvs)
    t_c4 = c.emit("vector", lambda e: e.reciprocal(col(PRD), col(CNT)), waits=[t_c3], inc=s_dvs)
    for t in range(NTILE):
        c.emit("vector", lambda e, t=t: e.tensor_scalar_mul(
            comb[:, t * 16:(t + 1) * 16], cmpa[:, t * 16:(t + 1) * 16],
            gst[:, PRD * NTILE + t:PRD * NTILE + t + 1]),
            waits=[t_c4] if t == 0 else [], inc=(s_dve if t == NTILE - 1 else None))
    t_comb_done = c.tok(s_dve)

    if debug:
        c.dma("scalar", dbg["x2"], x2s_d, s_dbg, waits=[(s_x2w, 16 * NTILE)])
        c.dma("scalar", dbg["comb"], comb[:], s_dbg, waits=[t_comb_done])
        c.dma("scalar", dbg["ln2T"], ln2Ts_d, s_dbg, waits=[(s_l2w, 16 * NSTRIP)])

    def esrc(e):
        if e < 16:
            return (wg_d[e], wu_d[e], wd_d[e], wgb_d[e], wub_d[e], wdb_d[e], e)
        hh = e - 16
        cs = slice(hh * I_MOE, (hh + 1) * I_MOE)
        return (swg_d[:, cs], swu_d[:, cs], swd_d[cs, :],
                swgb_d[hh], swub_d[hh], swdb_d[hh], None)

    t_p2_pe_done = c.tok(s_pe)
    t_l2w_all = (s_l2w, 16 * NSTRIP)
    rr = lambda ap: ap.rearrange("(c p) i -> p c i", p=128)
    rr_s = lambda t: t[:].rearrange("p (c i) -> p c i", c=NCH)

    t_gu_pe_done = None
    t_fin_done = None
    t_spills_g0 = None
    t_fin_roll = [None, None]
    t_out_roll = None
    n_out = 0
    t_outw = {}
    t_prev_u2 = None
    t_prev_silu2 = None
    t_prev_mult2 = None
    t_prev_d2 = None
    t_prev_ya_last = None
    t_wspill_g = {0: None, 1: None, 2: None}
    t_wspill_u = {0: None, 1: None, 2: None}
    t_dspill = {0: None, 1: None}
    t_slot_used = {0: None, 1: None, 2: None}
    gchunk = [0]

    for g in range(NG):
        w = [t_l2w_all, t_comb_done, t_p2_pe_done, t_gu_pe_done]
        src = ln2Ts_d[2 * g:2 * g + 2].rearrange("s c p f -> p s c f")
        t_lng = c.dma("sync", lnTg[:].rearrange("p (s c f) -> p s c f", s=GSTRIPS, c=NCH),
                      src, s_lng, waits=w)
        eorder = [16, 17] + list(range(16))
        for ei, e_ in enumerate(eorder):
            gsrc, usrc, dsrc, gbf, ubf, dbf, ce = esrc(e_)
            t_udone = {}
            t_silu_l = {}
            t_mult_l = {}
            nit = 0
            for ic in range(ICH):
                sl = gchunk[0] % 3
                gchunk[0] += 1
                w_ld = [t_slot_used[sl], t_wspill_g[sl], t_wspill_u[sl]]
                if (g, ei) == (0, 0) and ic < 2:
                    w_ld += [t_comb_done, t_l2w_all]
                if g == 1 and ei == 0 and ic < 2:
                    w_ld += t_spills_g0
                csl = slice(ic * 128, (ic + 1) * 128)
                if g == 0:
                    t_gl = c.dma("gpsimd", rr_s(wgq[sl]), rr(gsrc[:, csl]), s_wgc[sl], waits=w_ld)
                    t_ul = c.dma("gpsimd", rr_s(wuq[sl]), rr(usrc[:, csl]), s_wuc[sl])
                    t_wspill_g[sl] = c.dma("scalar", gbf[ic], wgq[sl][:], s_sg[sl],
                                           waits=[t_gl])
                    t_wspill_u[sl] = c.dma("scalar", ubf[ic], wuq[sl][:], s_su[sl],
                                           waits=[t_ul])
                else:
                    t_gl = c.dma("sync", wgq[sl][:], gbf[ic], s_wg[sl], waits=w_ld)
                    t_ul = c.dma("sync", wuq[sl][:], ubf[ic], s_wu[sl])
                for s in range(GSTRIPS):
                    ssl = nit % 2
                    w = [t_gl, t_lng,
                         t_silu_l.get(nit - 1) if nit >= 1 else t_prev_silu2]
                    for ci in range(NCH):
                        c.emit("tensor", lambda e, ci=ci, s=s, sl=sl: e.matmul(
                            ps_a[:], wgq[sl][:, ci * 128:(ci + 1) * 128],
                            lnTg[:, (s * NCH + ci) * 512:(s * NCH + ci) * 512 + 512],
                            start=(ci == 0), stop=(ci == NCH - 1)),
                            waits=w if ci == 0 else [], inc=(s_pe if ci == NCH - 1 else None))
                    t_gmm = c.tok(s_pe)
                    t_silu_l[nit] = c.emit(
                        "scalar", lambda e, ssl=ssl: e.activation(sgb[ssl][:], ps_a[:], AF.Silu),
                        waits=[t_gmm,
                               t_mult_l.get(nit - 2) if nit >= 2 else t_prev_mult2],
                        inc=s_act)
                    w = [t_ul, t_mult_l.get(nit - 1) if nit >= 1 else t_prev_mult2]
                    for ci in range(NCH):
                        c.emit("tensor", lambda e, ci=ci, s=s, sl=sl: e.matmul(
                            ps_b[:], wuq[sl][:, ci * 128:(ci + 1) * 128],
                            lnTg[:, (s * NCH + ci) * 512:(s * NCH + ci) * 512 + 512],
                            start=(ci == 0), stop=(ci == NCH - 1)),
                            waits=w if ci == 0 else [], inc=(s_pe if ci == NCH - 1 else None))
                    t_umm = c.tok(s_pe)
                    t_mult_l[nit] = c.emit(
                        "vector", lambda e, ic=ic, s=s, ssl=ssl: e.tensor_tensor(
                            ae[:, ic * GT + s * 512:ic * GT + s * 512 + 512],
                            sgb[ssl][:], ps_b[:], ALU.mult),
                        waits=[t_umm, t_silu_l[nit]], inc=s_dve)
                    if s == GSTRIPS - 1:
                        t_udone[ic] = t_mult_l[nit]
                        t_slot_used[sl] = t_mult_l[nit]
                    nit += 1
            t_prev_u2 = t_udone[ICH - 1]
            t_prev_silu2 = t_silu_l[nit - 1]
            t_prev_mult2 = t_mult_l[nit - 1]
            t_ae_done = t_udone[ICH - 1]
            t_dstop = {}
            t_ya = {}
            for dd in range(4):
                sl = dd % 2
                w_ld = [t_dstop.get(dd - 2) if dd >= 2 else t_prev_d2,
                        t_dspill[sl]]
                if (g, ei) == (0, 0) and dd < 2:
                    w_ld += [t_comb_done, t_l2w_all]
                if g == 1 and ei == 0 and dd < 2:
                    w_ld += t_spills_g0
                dsl = slice(dd * 512, (dd + 1) * 512)
                rrd = lambda ap: ap.rearrange("(c p) d -> p c d", p=128)
                rrd_s = lambda t: t[:].rearrange("p (c d) -> p c d", c=ICH)
                if g == 0:
                    t_dl = c.dma("gpsimd", rrd_s(wdq[sl]), rrd(dsrc[:, dsl]), s_wdc[sl], waits=w_ld)
                    t_dspill[sl] = c.dma("scalar", dbf[dd], wdq[sl][:], s_sd[sl],
                                         waits=[t_dl])
                else:
                    t_dl = c.dma("sync", wdq[sl][:], dbf[dd], s_wd[sl], waits=w_ld)
                for sub in range(GSUB):
                    k = dd * GSUB + sub
                    bank = ps_c if k % 2 == 0 else ps_d
                    if k >= 2:
                        w = [t_dl, t_ae_done, t_ya[k - 2]]
                    elif ei > 0:
                        w = [t_dl, t_ae_done, t_prev_ya_last]
                    else:
                        w = [t_dl, t_ae_done, t_fin_done]
                    for ic in range(ICH):
                        c.emit("tensor", lambda e, b=bank, ic=ic, sub=sub, sl=sl: e.matmul(
                            b[:], ae[:, ic * GT + sub * 128:ic * GT + sub * 128 + 128],
                            wdq[sl][:, ic * 512:(ic + 1) * 512],
                            start=(ic == 0), stop=(ic == ICH - 1)),
                            waits=w if ic == 0 else [], inc=(s_pe if ic == ICH - 1 else None))
                    t_dstop[dd] = c.tok(s_pe)
                    ysl = yg[:, sub * D + dd * 512:sub * D + dd * 512 + 512]
                    if ei == 0:
                        t_ya[k] = c.emit(
                            "vector", lambda e, b=bank, ysl=ysl: e.tensor_copy(ysl, b[:]),
                            waits=[t_dstop[dd]], inc=s_dve)
                    elif ce is None:
                        t_ya[k] = c.emit(
                            "vector", lambda e, b=bank, ysl=ysl: e.scalar_tensor_tensor(
                                ysl, b[:], 1.0, ysl, ALU.mult, ALU.add),
                            waits=[t_dstop[dd]], inc=s_dve)
                    else:
                        ccol = comb[:, (g * GSUB + sub) * 16 + ce:(g * GSUB + sub) * 16 + ce + 1]
                        t_ya[k] = c.emit(
                            "vector", lambda e, b=bank, ysl=ysl, ccol=ccol: e.scalar_tensor_tensor(
                                ysl, b[:], ccol, ysl, ALU.mult, ALU.add),
                            waits=[t_dstop[dd]], inc=s_dve)
            t_prev_d2 = t_dstop[3]
            t_prev_ya_last = t_ya[4 * GSUB - 1]
            if g == 0 and ei == len(eorder) - 1:
                t_spills_g0 = [c.tok(s) for s in (s_sg[0], s_sg[1], s_sg[2],
                                                  s_su[0], s_su[1], s_su[2],
                                                  s_sd[0], s_sd[1])]
        t_gu_pe_done = t_dstop[3]
        t_y_done = t_ya[4 * GSUB - 1]

        for sub in range(GSUB):
            sl = 0
            ti = g * GSUB + sub
            t_xfl = c.dma("sync", xf[sl][:], x2s_d[ti], s_xf[sl],
                          waits=[(s_x2w, 16 * NTILE), t_y_done, t_fin_roll[sl]])
            ysl = yg[:, sub * D:(sub + 1) * D]
            t_r = ln_stats(ysl, [t_y_done], dump3)
            t_n = c.emit("vector", lambda e, ysl=ysl: e.tensor_scalar(
                ysl, ysl, stats[:, 2:3], stats[:, 5:6], ALU.subtract, ALU.mult),
                waits=[t_r], inc=s_dvs)
            t_fin_roll[sl] = c.emit("vector", lambda e, ysl=ysl, sl=sl: e.tensor_tensor(
                outf[:], ysl, xf[sl][:], ALU.add),
                waits=[t_n, t_xfl, t_out_roll], inc=s_dve)
            t_out_roll = c.dma("scalar", out_d[ti], outf[:], s_ow,
                               waits=[t_fin_roll[sl]])
            t_outw[n_out] = t_out_roll
            n_out += 1
        t_fin_done = t_fin_roll[0]

    final = [c.tok(s) for s in (s_ow, s_x2w, s_l2w, s_zw, s_sg[0], s_sg[1], s_sg[2],
                                 s_su[0], s_su[1], s_su[2], s_sd[0], s_sd[1], s_dbg)]
    c.wait("sync", final)

    with nc.Block() as block:
        @block.tensor
        def _(e):
            for f in c.q["tensor"]:
                f(e)

        @block.vector
        def _(e):
            for f in c.q["vector"]:
                f(e)

        @block.scalar
        def _(e):
            for f in c.q["scalar"]:
                f(e)

        @block.gpsimd
        def _(e):
            for f in c.q["gpsimd"]:
                f(e)

        @block.sync
        def _(e):
            for f in c.q["sync"]:
                f(e)

    return nc


def host_consts():
    masks = np.zeros((128, 4 * 512), np.float32)
    for j in range(4):
        u = np.arange(128)[:, None]
        v = np.arange(512)[None, :]
        masks[:, j * 512:(j + 1) * 512] = np.where(v >= u + 128 * j, 0.0, -1e9)
    return {
        "masks": masks.astype(ml_dtypes.bfloat16),
        "ident": np.eye(128, dtype=ml_dtypes.bfloat16),
        "onesb": np.ones((128, 1), ml_dtypes.bfloat16),
        "rowones": np.ones((1, 128), np.float32),
        "ccols": np.tile(np.array([[EPS, -SHIFT]], np.float32), (128, 1)),
    }


def kernel(L0_attn_x, L0_attn_W_K, L0_attn_W_Q, L0_attn_W_V, L0_attn_W_O,
           L0_ffn_gate_w, L0_ffn_w_gate, L0_ffn_w_up, L0_ffn_w_down,
           L0_ffn_sw_gate, L0_ffn_sw_up, L0_ffn_sw_down):
    xs = np.ascontiguousarray(np.asarray(L0_attn_x, np.float32))
    shared = {
        "wqT": np.ascontiguousarray(np.asarray(L0_attn_W_Q, np.float32).transpose(0, 2, 1)),
        "wkT": np.ascontiguousarray(np.asarray(L0_attn_W_K, np.float32).transpose(0, 2, 1)),
        "wvT": np.ascontiguousarray(np.asarray(L0_attn_W_V, np.float32).transpose(0, 2, 1)),
        "woT": np.ascontiguousarray(np.asarray(L0_attn_W_O, np.float32).T),
        "gwT": np.ascontiguousarray(np.asarray(L0_ffn_gate_w, np.float32).T),
        "wg": np.ascontiguousarray(np.asarray(L0_ffn_w_gate, np.float32)),
        "wu": np.ascontiguousarray(np.asarray(L0_ffn_w_up, np.float32)),
        "wd": np.ascontiguousarray(np.asarray(L0_ffn_w_down, np.float32)),
        "swg": np.ascontiguousarray(np.asarray(L0_ffn_sw_gate, np.float32)),
        "swu": np.ascontiguousarray(np.asarray(L0_ffn_sw_up, np.float32)),
        "swd": np.ascontiguousarray(np.asarray(L0_ffn_sw_down, np.float32)),
        **host_consts(),
    }
    nc = build()
    in_maps = [{"x": xs[b], **shared} for b in range(8)]
    res = run_bass_kernel_spmd(nc, in_maps, core_ids=list(range(8)))
    out = np.stack([np.asarray(res.results[b]["out"], np.float32).reshape(S, D)
                    for b in range(8)])
    return out


# version 04
# speedup vs baseline: 1.0546x; 1.0092x over previous
import math
import numpy as np
import ml_dtypes
import concourse.bass as bass
import concourse.mybir as mybir
from concourse.bass_utils import run_bass_kernel_spmd

F32 = mybir.dt.float32
BF16 = mybir.dt.bfloat16
AF = mybir.ActivationFunctionType
ALU = mybir.AluOpType
AX = mybir.AxisListType

S, D, H, DH = 4096, 2048, 16, 128
NCH = D // 128
NTILE = S // 128
NSTRIP = S // 512
I_MOE = 1408
ICH = I_MOE // 128
NG = 4
GT = S // NG
GSTRIPS = GT // 512
GSUB = GT // 128
SCALE = 1.0 / math.sqrt(128.0)
SHIFT = 25.0
EPS = 1e-5
BIS_ITERS = 26

ENGS = ("tensor", "vector", "scalar", "gpsimd", "sync")


class Ctx:
    def __init__(self, nc):
        self.nc = nc
        self.q = {e: [] for e in ENGS}
        self.cnt = {}
        self.sems = {}
        self.lastw = {}

    def sem(self, name):
        h = self.nc.alloc_semaphore(name=name)
        self.sems[name] = h
        self.cnt[name] = 0
        return name

    def emit(self, eng, fn, waits=(), inc=None, amt=1):
        nw = []
        for t in waits:
            if t is None:
                continue
            s, v = t
            assert v <= self.cnt[s], f"forward wait {s} {v} > {self.cnt[s]}"
            if self.lastw.get((eng, s), -1) < v:
                nw.append((self.sems[s], v))
                self.lastw[(eng, s)] = v
        tok = None
        semh = None
        if inc is not None:
            self.cnt[inc] += amt
            tok = (inc, self.cnt[inc])
            semh = self.sems[inc]

        def play(e, nw=nw, fn=fn, semh=semh, amt=amt):
            for (hh, v) in nw:
                e.wait_ge(hh, v)
            ins = fn(e)
            if semh is not None:
                ins.then_inc(semh, amt)
        self.q[eng].append(play)
        return tok

    def wait(self, eng, waits):
        self.emit(eng, lambda e: e.nop(), waits=waits)

    def dma(self, eng, out, in_, sem, waits=()):
        waits = list(waits) + [(sem, self.cnt[sem])]
        return self.emit(eng, lambda e, o=out, i=in_: e.dma_start(out=o, in_=i),
                         waits=waits, inc=sem, amt=16)

    def tok(self, sem):
        return (sem, self.cnt[sem])


def build(debug=False):
    nc = bass.Bass("TRN2", target_bir_lowering=False)

    x_d = nc.dram_tensor("x", [S, D], F32, kind="ExternalInput").ap()
    wqT_d = nc.dram_tensor("wqT", [H, D, DH], F32, kind="ExternalInput").ap()
    wkT_d = nc.dram_tensor("wkT", [H, D, DH], F32, kind="ExternalInput").ap()
    wvT_d = nc.dram_tensor("wvT", [H, D, DH], F32, kind="ExternalInput").ap()
    woT_d = nc.dram_tensor("woT", [D, D], F32, kind="ExternalInput").ap()
    gwT_d = nc.dram_tensor("gwT", [D, 16], F32, kind="ExternalInput").ap()
    wg_d = nc.dram_tensor("wg", [16, D, I_MOE], F32, kind="ExternalInput").ap()
    wu_d = nc.dram_tensor("wu", [16, D, I_MOE], F32, kind="ExternalInput").ap()
    wd_d = nc.dram_tensor("wd", [16, I_MOE, D], F32, kind="ExternalInput").ap()
    swg_d = nc.dram_tensor("swg", [D, 2 * I_MOE], F32, kind="ExternalInput").ap()
    swu_d = nc.dram_tensor("swu", [D, 2 * I_MOE], F32, kind="ExternalInput").ap()
    swd_d = nc.dram_tensor("swd", [2 * I_MOE, D], F32, kind="ExternalInput").ap()
    masks_d = nc.dram_tensor("masks", [128, 4 * 512], BF16, kind="ExternalInput").ap()
    ident_d = nc.dram_tensor("ident", [128, 128], BF16, kind="ExternalInput").ap()
    onesb_d = nc.dram_tensor("onesb", [128, 1], BF16, kind="ExternalInput").ap()
    rowones_d = nc.dram_tensor("rowones", [1, 128], F32, kind="ExternalInput").ap()
    ccols_d = nc.dram_tensor("ccols", [128, 2], F32, kind="ExternalInput").ap()

    out_d = nc.dram_tensor("out", [NTILE, 128, D], F32, kind="ExternalOutput").ap()

    x2s_d = nc.dram_tensor("x2s", [NTILE, 128, D], F32, kind="Internal").ap()
    ln2Ts_d = nc.dram_tensor("ln2Ts", [NSTRIP, NCH, 128, 512], BF16, kind="Internal").ap()
    zs_d = nc.dram_tensor("zs", [H, 128, S], BF16, kind="Internal").ap()
    wgb_d = nc.dram_tensor("wgb", [16, ICH, 128, NCH * 128], BF16, kind="Internal").ap()
    wub_d = nc.dram_tensor("wub", [16, ICH, 128, NCH * 128], BF16, kind="Internal").ap()
    wdb_d = nc.dram_tensor("wdb", [16, 4, 128, ICH * 512], BF16, kind="Internal").ap()
    swgb_d = nc.dram_tensor("swgb", [2, ICH, 128, NCH * 128], BF16, kind="Internal").ap()
    swub_d = nc.dram_tensor("swub", [2, ICH, 128, NCH * 128], BF16, kind="Internal").ap()
    swdb_d = nc.dram_tensor("swdb", [2, 4, 128, ICH * 512], BF16, kind="Internal").ap()

    dbg = {}
    if debug:
        dbg["ln1T"] = nc.dram_tensor("dbg_ln1T", [128, NCH * S], BF16, kind="ExternalOutput").ap()
        dbg["z"] = nc.dram_tensor("dbg_z", [H, 128, S], BF16, kind="ExternalOutput").ap()
        dbg["x2"] = nc.dram_tensor("dbg_x2", [NTILE, 128, D], F32, kind="ExternalOutput").ap()
        dbg["comb"] = nc.dram_tensor("dbg_comb", [128, NTILE * 16], F32, kind="ExternalOutput").ap()
        dbg["ln2T"] = nc.dram_tensor("dbg_ln2T", [NSTRIP, NCH, 128, 512], BF16, kind="ExternalOutput").ap()

    A = nc.alloc_sbuf_tensor
    ident = A("c_ident", [128, 128], BF16)
    masks = A("c_masks", [128, 4 * 512], BF16)
    onesb = A("c_onesb", [128, 1], BF16)
    rowones = A("c_rowones", [1, 128], F32)
    ccols = A("c_ccols", [128, 2], F32)
    gwT_sb = A("gwT_sb", [128, 16 * 16], BF16)
    comb = A("comb", [128, NTILE * 16], F32)
    probs = A("probs", [128, NTILE * 16], F32)
    stats = A("stats", [128, 16], F32)
    rsum = A("rsum", [1, 512], F32)
    eps_c = ccols[:, 0:1]
    shm_c = ccols[:, 1:2]

    A_BYTES = 131072
    B_BYTES = 68000
    top = nc.sbuf_top
    base_A = top - A_BYTES
    base_B = base_A - B_BYTES
    assert nc.sbuf_base + 64 < base_B, (nc.sbuf_base, base_B)

    def at(name, shape, dtype, off):
        nbytes = shape[1] * (2 if dtype == BF16 else 4)
        assert off + nbytes <= top, (name, off, nbytes)
        return nc.alloc_sbuf_tensor_at(name, shape, dtype, offset=off)

    ln1T = at("ln1T", [128, NCH * S], BF16, base_A)
    woT_sb = at("woT_sb", [128, NCH * D], BF16, base_A)
    zsb = [at(f"zsb{i}", [128, H * 512], BF16, base_A + 65536 + i * 16384) for i in range(2)]
    ao = at("ao", [128, D], F32, base_A + 98304)
    x2st = at("x2st", [128, D], F32, base_A + 106496)
    yg = at("yg", [128, GSUB * D], F32, base_A)
    lnTg = at("lnTg", [128, GSTRIPS * NCH * 512], BF16, base_A + 65536)
    ae = at("ae", [128, ICH * GT], BF16, base_A + 98304)
    dump3 = at("dump3", [128, D], F32, base_A + 120832)

    xt = [at(f"xt{i}", [128, D], F32, base_B + i * 8192) for i in range(2)]
    dump0 = at("dump0", [128, D], F32, base_B + 32768)
    dump2 = at("dump2", [128, D], F32, base_B + 43008)
    ln1b = at("ln1b", [128, D], BF16, base_B + 24576)
    wqkv = [at(f"wqkv{i}", [128, 3 * NCH * DH], BF16, base_B + i * 12288) for i in range(2)]
    Qf = at("Qf", [128, S], BF16, base_B + 24576)
    Kf = at("Kf", [128, S], BF16, base_B + 32768)
    Vt = at("Vt", [128, S], BF16, base_B + 40960)
    vtmp = at("vtmp", [128, 512], BF16, base_B + 49152)
    zh = at("zh", [128, S], BF16, base_B + 50176)
    atb = [at(f"atb{i}", [128, 512], BF16, base_B + 58368 + i * 1024) for i in range(3)]
    bc_sb = at("bc_sb", [128, 512], F32, base_B + 61440)
    xin = [at(f"xin{i}", [128, D], F32, base_B + i * 8192) for i in range(2)]
    ln2b = at("ln2b", [128, D], BF16, base_B + 16384)
    l2st = at("l2st", [128, NCH * 512], BF16, base_B + 20480)
    ptmp = at("ptmp", [128, 16], F32, base_B + 36864)
    gst = at("gst", [128, 7 * NTILE], F32, base_B)
    cmpa = at("cmpa", [128, NTILE * 16], F32, base_B + 896)
    wgq = [at(f"wgq{i}", [128, NCH * 128], BF16, base_B + i * 4096) for i in range(3)]
    wuq = [at(f"wuq{i}", [128, NCH * 128], BF16, base_B + 12288 + i * 4096) for i in range(3)]
    wdq = [at(f"wdq{i}", [128, ICH * 512], BF16, base_B + 24576 + i * 11264) for i in range(2)]
    sgb = [at(f"sgb{i}", [128, 512], F32, base_B + 47104 + i * 2048) for i in range(2)]
    xf = [at(f"xf{i}", [128, D], F32, base_B + 51200) for i in range(1)]
    outf = at("outf", [128, D], F32, base_B + 59392)

    ps_a = nc.alloc_psum_tensor("ps_a", [128, 512], F32)
    ps_b = nc.alloc_psum_tensor("ps_b", [128, 512], F32)
    ps_c = nc.alloc_psum_tensor("ps_c", [128, 512], F32)
    ps_d = nc.alloc_psum_tensor("ps_d", [128, 512], F32)
    ps_sum = nc.alloc_psum_tensor("ps_sum", [1, 512], F32)
    ps_bc = nc.alloc_psum_tensor("ps_bc", [128, 512], F32)
    ps_t1 = nc.alloc_psum_tensor("ps_t1", [128, 1024], BF16)
    ps_t2 = nc.alloc_psum_tensor("ps_t2", [128, 1024], BF16)

    c = Ctx(nc)
    s_pe = c.sem("s_pe")
    s_act = c.sem("s_act")
    s_dve = c.sem("s_dve")
    s_dvs = c.sem("s_dvs")
    s_acs = c.sem("s_acs")
    s_ci = c.sem("s_ci")
    s_gw = c.sem("s_gw")
    s_x = [c.sem("s_x0"), c.sem("s_x1")]
    s_wq = [c.sem("s_wq0"), c.sem("s_wq1")]
    s_zw = c.sem("s_zw")
    s_zs = [c.sem("s_zs0"), c.sem("s_zs1")]
    s_x2w = c.sem("s_x2w")
    s_l2w = c.sem("s_l2w")
    s_wo = c.sem("s_wo")
    s_lng = c.sem("s_lng")
    s_wg = [c.sem("s_wg0"), c.sem("s_wg1"), c.sem("s_wg2")]
    s_wu = [c.sem("s_wu0"), c.sem("s_wu1"), c.sem("s_wu2")]
    s_wd = [c.sem("s_wd0"), c.sem("s_wd1")]
    s_wgc = [c.sem("s_wgc0"), c.sem("s_wgc1"), c.sem("s_wgc2")]
    s_wuc = [c.sem("s_wuc0"), c.sem("s_wuc1"), c.sem("s_wuc2")]
    s_wdc = [c.sem("s_wdc0"), c.sem("s_wdc1")]
    s_sg = [c.sem("s_sg0"), c.sem("s_sg1"), c.sem("s_sg2")]
    s_su = [c.sem("s_su0"), c.sem("s_su1"), c.sem("s_su2")]
    s_sd = [c.sem("s_sd0"), c.sem("s_sd1")]
    s_xf = [c.sem("s_xf0"), c.sem("s_xf1")]
    s_ow = c.sem("s_ow")
    s_dbg = c.sem("s_dbg")

    ln1T_r = ln1T[:].rearrange("p (c t) -> p c t", c=NCH)

    c.dma("sync", ident[:], ident_d, s_ci)
    c.dma("sync", masks[:], masks_d, s_ci)
    c.dma("sync", onesb[:], onesb_d, s_ci)
    c.dma("sync", rowones[:], rowones_d, s_ci)
    c.dma("sync", ccols[:], ccols_d, s_ci)
    c.dma("gpsimd", gwT_sb[:].rearrange("p (c e) -> p c e", c=NCH),
          gwT_d.rearrange("(c p) e -> p c e", p=128), s_gw)
    t_init_in = c.tok(s_ci)
    t_init_cst = c.tok(s_gw)

    def ln_stats(src_ap, waits, dmp, cb=0):
        sc = lambda k: stats[:, cb + k:cb + k + 1]
        t1 = c.emit("scalar", lambda e, a=src_ap, d=dmp: e.activation(
            d[:], a, AF.Identity, accum_out=sc(0)), waits=waits, inc=s_act)
        t2 = c.emit("scalar", lambda e, a=src_ap, d=dmp: e.activation(
            d[:], a, AF.Square, accum_out=sc(1)), waits=[t1], inc=s_act)
        c.emit("vector", lambda e: e.tensor_scalar_mul(sc(2), sc(0), 1.0 / D),
               waits=[t2], inc=s_dve)
        t3 = c.emit("vector", lambda e: e.tensor_scalar_mul(sc(3), sc(1), 1.0 / D),
                    inc=s_dvs)
        t4 = c.emit("vector", lambda e: e.tensor_tensor(sc(4), sc(2), sc(2), ALU.mult),
                    waits=[t3], inc=s_dvs)
        t5 = c.emit("vector", lambda e: e.tensor_tensor(sc(3), sc(3), sc(4), ALU.subtract),
                    waits=[t4], inc=s_dve)
        t6 = c.emit("scalar", lambda e: e.activation(sc(5), sc(3), AF.Ln, bias=eps_c),
                    waits=[t5], inc=s_acs)
        t7 = c.emit("scalar", lambda e: e.activation(sc(5), sc(5), AF.Exp, scale=-0.5),
                    waits=[t6], inc=s_act)
        return t7

    t_apply = {}
    t_evac = {}
    t_trdone = {}
    t_xload = [None, None]
    for i in range(NTILE):
        sl = i % 2
        w = [t_apply.get(i - 2)]
        t_xload[sl] = c.dma("sync", xt[sl][:], x_d[i * 128:(i + 1) * 128, :], s_x[sl], waits=w)
        t_r = ln_stats(xt[sl][:], [t_xload[sl], t_init_in], dump0)
        w = [t_r]
        if i >= 1:
            w.append(t_trdone[2 * i - 1])
        t_apply[i] = c.emit("vector", lambda e, sl=sl: e.tensor_scalar(
            ln1b[:], xt[sl][:], stats[:, 2:3], stats[:, 5:6], ALU.subtract, ALU.mult),
            waits=w, inc=s_dve)
        for half in range(2):
            bank = ps_t1 if (i * 2 + half) % 2 == 0 else ps_t2
            w = [t_apply[i], t_evac.get((i * 2 + half) - 2)]
            for k in range(8):
                ch = half * 8 + k
                c.emit("tensor", lambda e, b=bank, k=k, ch=ch: e.transpose(
                    b[:, k * 128:(k + 1) * 128], ln1b[:, ch * 128:(ch + 1) * 128], ident[:]),
                    waits=w if k == 0 else [], inc=(s_pe if k == 7 else None))
            t_trdone[i * 2 + half] = c.tok(s_pe)
            dst = ln1T_r[:, half * 8:(half + 1) * 8, i * 128:(i + 1) * 128]
            t_evac[i * 2 + half] = c.emit(
                "vector", lambda e, b=bank, d=dst: e.tensor_copy(d, b[:]),
                waits=[t_trdone[i * 2 + half]], inc=s_dve)
    t_ln1T_done = c.tok(s_dve)

    if debug:
        c.dma("scalar", dbg["ln1T"], ln1T[:], s_dbg, waits=[t_ln1T_done])

    t_qkv_used = {}
    tail_box = [None]
    t_zdma = {}
    t_zmul = {}
    t_recip = {}
    t_bccopy = {}

    for h in range(H):
        sl = h % 2
        w = [t_ln1T_done if h == 0 else None, t_qkv_used.get(h - 2)]
        buf = wqkv[sl]
        t_w = None
        for mi, wsrc in enumerate((wqT_d, wkT_d, wvT_d)):
            src = wsrc[h].rearrange("(c p) m -> p c m", p=128)
            dst = buf[:, mi * 2048:(mi + 1) * 2048].rearrange("p (c m) -> p c m", c=NCH)
            t_w = c.dma("gpsimd", dst, src, s_wq[sl],
                        waits=w if mi == 0 else [])
        t_vt_done = None
        t_qkv_evac = {}
        for mi in range(3):
            dstt = (Qf, Kf, None)[mi]
            for s in range(NSTRIP):
                bank = ps_a if s % 2 == 0 else ps_b
                w = [t_w]
                if h > 0 and mi == 0 and s == 0:
                    w.append(t_zmul[(h - 1, NSTRIP - 1)])
                for ci in range(NCH):
                    c.emit("tensor", lambda e, b=bank, buf=buf, mi=mi, ci=ci, s=s: e.matmul(
                        b[:], buf[:, mi * 2048 + ci * 128:mi * 2048 + (ci + 1) * 128],
                        ln1T[:, ci * S + s * 512:ci * S + (s + 1) * 512],
                        start=(ci == 0), stop=(ci == NCH - 1)),
                        waits=w if ci == 0 else [],
                        inc=(s_pe if ci == NCH - 1 else None))
                t_mm = c.tok(s_pe)
                if mi < 2:
                    t_qkv_evac[(mi, s)] = c.emit(
                        "scalar", lambda e, d=dstt, b=bank, s=s: e.copy(
                            d[:, s * 512:(s + 1) * 512], b[:]),
                        waits=[t_mm], inc=s_act)
                else:
                    t_ev = c.emit("scalar", lambda e, b=bank: e.copy(vtmp[:], b[:]),
                                  waits=[t_mm], inc=s_act)
                    tb = ps_t1 if s % 2 == 0 else ps_t2
                    for k in range(4):
                        c.emit("tensor", lambda e, tb=tb, k=k: e.transpose(
                            tb[:, k * 128:(k + 1) * 128], vtmp[:, k * 128:(k + 1) * 128], ident[:]),
                            waits=[t_ev] if k == 0 else [], inc=(s_pe if k == 3 else None))
                    t_tr = c.tok(s_pe)
                    t_vt_done = c.emit("vector", lambda e, tb=tb, s=s: e.tensor_copy(
                        Vt[:, s * 512:(s + 1) * 512], tb[:, 0:512]),
                        waits=[t_tr], inc=s_dve)
        t_qkv_used[h] = c.tok(s_pe)
        t_k_done = t_qkv_evac[(1, NSTRIP - 1)]
        t_q_done = t_qkv_evac[(0, NSTRIP - 1)]

        for qs in range(NSTRIP):
            npc = 4 * qs + 4
            zbank = ps_c if qs % 2 == 0 else ps_d
            zoth = ps_d if qs % 2 == 0 else ps_c
            sring = [ps_a, ps_b, zoth]
            t_exp = {}
            t_scores = {}
            last_avstop = {}

            def emit_score(pc, qs=qs, npc=npc, sring=sring):
                bank = sring[pc % 3]
                w = [t_k_done, t_q_done, t_exp.get(pc - 3)]
                if qs == 0 and pc <= 1:
                    w.append(t_vt_done)
                if pc % 3 == 2 and pc == 2:
                    pk = (h, qs - 1) if qs > 0 else (h - 1, NSTRIP - 1)
                    if pk in t_zmul:
                        w.append(t_zmul[pk])
                t_scores[pc] = c.emit("tensor", lambda e, b=bank, pc=pc, qs=qs: e.matmul(
                    b[:], Kf[:, pc * 128:(pc + 1) * 128], Qf[:, qs * 512:(qs + 1) * 512],
                    start=True, stop=True), waits=w, inc=s_pe)

            emit_score(0)
            if npc > 1:
                emit_score(1)
            if tail_box[0] is not None:
                tail_box[0]()
                tail_box[0] = None
            for pc in range(npc):
                bank = sring[pc % 3]
                wexp = [t_scores[pc], last_avstop.get(pc - 3)]
                if pc >= 4 * qs:
                    j = pc - 4 * qs
                    t_msk = c.emit("vector", lambda e, b=bank, j=j: e.tensor_tensor(
                        b[:], b[:], masks[:, j * 512:(j + 1) * 512], ALU.add),
                        waits=[t_scores[pc]], inc=s_dve)
                    wexp[0] = t_msk
                ab = atb[pc % 3]
                t_exp[pc] = c.emit("scalar", lambda e, ab=ab, b=bank: e.activation(
                    ab[:], b[:], AF.Exp, bias=shm_c, scale=SCALE),
                    waits=wexp, inc=s_act)
                if pc + 2 < npc:
                    emit_score(pc + 2)
                w_s = [t_exp[pc]]
                if pc == 0 and (h > 0 or qs > 0):
                    w_s.append(t_recip[(h, qs - 1) if qs > 0 else (h - 1, NSTRIP - 1)])
                c.emit("tensor", lambda e, ab=ab, pc=pc, npc=npc: e.matmul(
                    ps_sum[:], onesb[:], ab[:], start=(pc == 0), stop=(pc == npc - 1)),
                    waits=w_s, inc=(s_pe if pc == npc - 1 else None))
                w_z = [t_vt_done]
                if pc == 0 and (h * NSTRIP + qs) >= 2:
                    pqs = qs - 2 if qs >= 2 else qs + NSTRIP - 2
                    ph = h if qs >= 2 else h - 1
                    w_z.append(t_zmul[(ph, pqs)])
                last_avstop[pc] = c.emit("tensor", lambda e, ab=ab, zb=zbank, pc=pc, npc=npc:
                                         e.matmul(zb[:], Vt[:, pc * 128:(pc + 1) * 128], ab[:],
                                                  start=(pc == 0), stop=(pc == npc - 1)),
                                         waits=w_z, inc=s_pe)
            t_sum_stop = (s_pe, c.cnt[s_pe] - 1)
            t_av_stop = last_avstop[npc - 1]
            t_recip[(h, qs)] = c.emit("vector", lambda e: e.reciprocal(rsum[:], ps_sum[:]),
                                      waits=[t_sum_stop], inc=s_dve)

            def tail(h=h, qs=qs, zbank=zbank, t_av_stop=t_av_stop):
                w_bc = [t_recip[(h, qs)], t_init_in]
                if (h, qs) != (0, 0):
                    pk = (h, qs - 1) if qs > 0 else (h - 1, NSTRIP - 1)
                    w_bc += [t_bccopy[pk], t_zmul[pk]]
                t_bcmm = c.emit("tensor", lambda e: e.matmul(ps_bc[:], rowones[:], rsum[:],
                                                             start=True, stop=True),
                                waits=w_bc, inc=s_pe)
                t_bccopy[(h, qs)] = c.emit("scalar", lambda e: e.copy(bc_sb[:], ps_bc[:]),
                                           waits=[t_bcmm], inc=s_act)
                w_zm = [t_bccopy[(h, qs)], t_av_stop]
                if h > 0 and qs == 0:
                    w_zm.append(t_zdma[h - 1])
                t_zmul[(h, qs)] = c.emit(
                    "vector", lambda e, qs=qs, zb=zbank:
                    e.tensor_tensor(zh[:, qs * 512:(qs + 1) * 512], zb[:], bc_sb[:], ALU.mult),
                    waits=w_zm, inc=s_dve)
            tail_box[0] = tail
        tail_box[0]()
        tail_box[0] = None
        t_zdma[h] = c.dma("scalar", zs_d[h], zh[:], s_zw, waits=[t_zmul[(h, NSTRIP - 1)]])

    t_p1_pe_done = c.tok(s_pe)
    t_p1_dve_done = c.tok(s_dve)
    t_zdma_all = c.tok(s_zw)

    if debug:
        c.dma("scalar", dbg["z"], zs_d, s_dbg, waits=[t_zdma_all])

    t_wo = c.dma("gpsimd", woT_sb[:].rearrange("p (c d) -> p c d", c=NCH),
                 woT_d.rearrange("(c p) d -> p c d", p=128),
                 s_wo, waits=[t_p1_pe_done])

    t_zs_used = {}
    t_zsl = [None, None]
    t_x2add = {}
    t_x2w_l = {}
    t_gexp = {}
    t_probs = {}
    t_l2w = {}
    t_l2ev = {}
    t_l2b_prev = [None]
    t_tr2done_l = [None]
    zs_r = zs_d.rearrange("h p (s f) -> s p h f", s=NSTRIP)

    deferred = [None]

    def emit_deferred():
        if deferred[0] is None:
            return
        deferred[0]()
        deferred[0] = None

    for t in range(NTILE):
        st = t // 4
        sl = st % 2
        sl2 = t % 2
        if t % 4 == 0:
            w = [t_zdma_all, t_p1_dve_done, t_zs_used.get(st - 2)]
            t_zsl[sl] = c.dma("sync", zsb[sl][:].rearrange("p (h f) -> p h f", h=H),
                              zs_r[st], s_zs[sl], waits=w)
        t_xl = c.dma("sync", xin[sl2][:], x_d[t * 128:(t + 1) * 128, :], s_x[sl2],
                     waits=[t_p1_dve_done, t_x2add.get(t - 2)])
        t_wod = {}
        for dd in range(4):
            bank = ps_a if dd % 2 == 0 else ps_b
            w = [t_wo, t_zsl[sl], t_x2add.get(t - 1)]
            for ci in range(NCH):
                c.emit("tensor", lambda e, b=bank, ci=ci, dd=dd, t=t, sl=sl: e.matmul(
                    b[:], zsb[sl][:, ci * 512 + (t % 4) * 128:ci * 512 + (t % 4) * 128 + 128],
                    woT_sb[:, ci * D + dd * 512:ci * D + dd * 512 + 512],
                    start=(ci == 0), stop=(ci == NCH - 1)),
                    waits=w if ci == 0 else [], inc=(s_pe if ci == NCH - 1 else None))
            t_mm = c.tok(s_pe)
            t_wod[dd] = c.emit("scalar", lambda e, b=bank, dd=dd: e.copy(
                ao[:, dd * 512:(dd + 1) * 512], b[:]), waits=[t_mm], inc=s_act)
            if dd == 1:
                emit_deferred()
        t_zs_used[st] = c.tok(s_pe)
        t_r = ln_stats(ao[:], [t_wod[3]], dump2)
        t_norm = c.emit("vector", lambda e: e.tensor_scalar(
            ao[:], ao[:], stats[:, 2:3], stats[:, 5:6], ALU.subtract, ALU.mult),
            waits=[t_r], inc=s_dvs)
        t_x2add[t] = c.emit("vector", lambda e, sl2=sl2: e.tensor_tensor(
            x2st[:], ao[:], xin[sl2][:], ALU.add),
            waits=[t_norm, t_xl, t_x2w_l.get(t - 1), t_l2b_prev[0]], inc=s_dve)
        t_x2w_l[t] = c.dma("scalar", x2s_d[t], x2st[:], s_x2w, waits=[t_x2add[t]])
        t_r2 = ln_stats(x2st[:], [t_x2add[t], t_l2b_prev[0]], dump2, cb=8)

        def make_deferred(t=t, st=st, t_r2=t_r2):
            def go():
                t_l2b = c.emit("vector", lambda e: e.tensor_scalar(
                    ln2b[:], x2st[:], stats[:, 10:11], stats[:, 13:14],
                    ALU.subtract, ALU.mult),
                    waits=[t_r2, t_tr2done_l[0]], inc=s_dve)
                t_l2b_prev[0] = t_l2b
                for half in range(2):
                    bank = ps_t1 if half == 0 else ps_t2
                    w = [t_l2b, t_l2ev.get(half)]
                    for k in range(8):
                        ch = half * 8 + k
                        c.emit("tensor", lambda e, b=bank, k=k, ch=ch: e.transpose(
                            b[:, k * 128:(k + 1) * 128], ln2b[:, ch * 128:(ch + 1) * 128],
                            ident[:]),
                            waits=w if k == 0 else [], inc=(s_pe if k == 7 else None))
                    t_trh = c.tok(s_pe)
                    if half == 1:
                        t_tr2done_l[0] = t_trh
                    dst = l2st[:].rearrange("p (c f) -> p c f", c=NCH)[
                        :, half * 8:(half + 1) * 8, (t % 4) * 128:(t % 4) * 128 + 128]
                    wv = [t_trh]
                    if st > 0 and t % 4 == 0:
                        wv.append(t_l2w[st - 1])
                    t_l2ev[half] = c.emit("vector", lambda e, b=bank, d=dst: e.tensor_copy(d, b[:]),
                                          waits=wv, inc=s_dve)
                w = [t_l2ev[0], t_l2ev[1], t_init_cst, t_gexp.get(t - 1)]
                for ci in range(NCH):
                    c.emit("tensor", lambda e, ci=ci, t=t: e.matmul(
                        ps_bc[:, 0:16],
                        l2st[:, ci * 512 + (t % 4) * 128:ci * 512 + (t % 4) * 128 + 128],
                        gwT_sb[:, ci * 16:(ci + 1) * 16],
                        start=(ci == 0), stop=(ci == NCH - 1)),
                        waits=w if ci == 0 else [], inc=(s_pe if ci == NCH - 1 else None))
                t_gmm = c.tok(s_pe)
                t_nm = c.emit("vector", lambda e: e.reduce_max(stats[:, 6:7], ps_bc[:, 0:16],
                                                               axis=AX.X, negate=True),
                              waits=[t_gmm], inc=s_dve)
                t_gexp[t] = c.emit("scalar", lambda e: e.activation(ptmp[:], ps_bc[:, 0:16],
                                                                    AF.Exp, bias=stats[:, 6:7]),
                                   waits=[t_nm], inc=s_act)
                t_rs = c.emit("vector", lambda e: e.reduce_sum(stats[:, 7:8], ptmp[:], axis=AX.X),
                              waits=[t_gexp[t]], inc=s_dvs)
                t_rw = c.emit("vector", lambda e: e.reciprocal(stats[:, 14:15], stats[:, 7:8]),
                              waits=[t_rs], inc=s_dvs)
                t_probs[t] = c.emit("vector", lambda e, t=t: e.tensor_scalar_mul(
                    probs[:, t * 16:(t + 1) * 16], ptmp[:], stats[:, 14:15]),
                    waits=[t_rw], inc=s_dve)
                if t % 4 == 3:
                    t_l2w[st] = c.dma("scalar", ln2Ts_d[st].rearrange("c p f -> p c f"),
                                      l2st[:].rearrange("p (c f) -> p c f", c=NCH),
                                      s_l2w, waits=[t_l2ev[0], t_l2ev[1]])
            return go
        deferred[0] = make_deferred()
    emit_deferred()

    LO, HI, CNT, PRD, MID, D1, D2 = range(7)
    col = lambda k: gst[:, k * NTILE:(k + 1) * NTILE]
    c.emit("vector", lambda e: e.memset(col(LO), 0.0),
           waits=[t_probs[NTILE - 1]], inc=s_dve)
    for t in range(NTILE):
        c.emit("vector", lambda e, t=t: e.reduce_max(
            gst[:, HI * NTILE + t:HI * NTILE + t + 1],
            probs[:, t * 16:(t + 1) * 16], axis=AX.X),
            inc=(s_dvs if t == NTILE - 1 else None))
    t_bis = c.tok(s_dvs)
    for it in range(BIS_ITERS):
        t_m1 = c.emit("vector", lambda e: e.tensor_tensor(col(MID), col(LO), col(HI), ALU.add),
                      waits=[t_bis], inc=s_dvs)
        t_m2 = c.emit("vector", lambda e: e.tensor_scalar_mul(col(MID), col(MID), 0.5),
                      waits=[t_m1], inc=s_dvs)
        for t in range(NTILE):
            c.emit("vector", lambda e, t=t: e.tensor_scalar(
                cmpa[:, t * 16:(t + 1) * 16], probs[:, t * 16:(t + 1) * 16],
                gst[:, MID * NTILE + t:MID * NTILE + t + 1], None, ALU.is_ge),
                waits=[t_m2] if t == 0 else [], inc=(s_dvs if t == NTILE - 1 else None))
        t_cmp = c.tok(s_dvs)
        for t in range(NTILE):
            c.emit("vector", lambda e, t=t: e.reduce_sum(
                gst[:, CNT * NTILE + t:CNT * NTILE + t + 1],
                cmpa[:, t * 16:(t + 1) * 16], axis=AX.X),
                waits=[t_cmp] if t == 0 else [], inc=(s_dvs if t == NTILE - 1 else None))
        t_cnt = c.tok(s_dvs)
        t_prd = c.emit("vector", lambda e: e.tensor_scalar(col(PRD), col(CNT), 4.0, None, ALU.is_ge),
                       waits=[t_cnt], inc=s_dvs)
        c.emit("vector", lambda e: e.tensor_tensor(col(D1), col(MID), col(LO), ALU.subtract),
               waits=[t_prd])
        t_d2 = c.emit("vector", lambda e: e.tensor_tensor(col(D2), col(HI), col(MID), ALU.subtract),
                      inc=s_dvs)
        c.emit("vector", lambda e: e.tensor_tensor(col(D1), col(D1), col(PRD), ALU.mult),
               waits=[t_d2])
        t_m3 = c.emit("vector", lambda e: e.tensor_tensor(col(D2), col(D2), col(PRD), ALU.mult),
                      inc=s_dvs)
        c.emit("vector", lambda e: e.tensor_tensor(col(LO), col(LO), col(D1), ALU.add),
               waits=[t_m3])
        t_bis = c.emit("vector", lambda e: e.tensor_tensor(col(HI), col(MID), col(D2), ALU.add),
                       inc=s_dvs)
    for t in range(NTILE):
        c.emit("vector", lambda e, t=t: e.tensor_scalar(
            cmpa[:, t * 16:(t + 1) * 16], probs[:, t * 16:(t + 1) * 16],
            gst[:, LO * NTILE + t:LO * NTILE + t + 1], None, ALU.is_ge),
            waits=[t_bis] if t == 0 else [], inc=(s_dvs if t == NTILE - 1 else None))
    t_c1 = c.tok(s_dvs)
    for t in range(NTILE):
        c.emit("vector", lambda e, t=t: e.tensor_tensor(
            cmpa[:, t * 16:(t + 1) * 16], cmpa[:, t * 16:(t + 1) * 16],
            probs[:, t * 16:(t + 1) * 16], ALU.mult),
            waits=[t_c1] if t == 0 else [], inc=(s_dvs if t == NTILE - 1 else None))
    t_c2 = c.tok(s_dvs)
    for t in range(NTILE):
        c.emit("vector", lambda e, t=t: e.reduce_sum(
            gst[:, CNT * NTILE + t:CNT * NTILE + t + 1],
            cmpa[:, t * 16:(t + 1) * 16], axis=AX.X),
            waits=[t_c2] if t == 0 else [], inc=(s_dvs if t == NTILE - 1 else None))
    t_c3 = c.tok(s_dvs)
    t_c4 = c.emit("vector", lambda e: e.reciprocal(col(PRD), col(CNT)), waits=[t_c3], inc=s_dvs)
    for t in range(NTILE):
        c.emit("vector", lambda e, t=t: e.tensor_scalar_mul(
            comb[:, t * 16:(t + 1) * 16], cmpa[:, t * 16:(t + 1) * 16],
            gst[:, PRD * NTILE + t:PRD * NTILE + t + 1]),
            waits=[t_c4] if t == 0 else [], inc=(s_dve if t == NTILE - 1 else None))
    t_comb_done = c.tok(s_dve)

    if debug:
        c.dma("scalar", dbg["x2"], x2s_d, s_dbg, waits=[(s_x2w, 16 * NTILE)])
        c.dma("scalar", dbg["comb"], comb[:], s_dbg, waits=[t_comb_done])
        c.dma("scalar", dbg["ln2T"], ln2Ts_d, s_dbg, waits=[(s_l2w, 16 * NSTRIP)])

    def esrc(e):
        if e < 16:
            return (wg_d[e], wu_d[e], wd_d[e], wgb_d[e], wub_d[e], wdb_d[e], e)
        hh = e - 16
        cs = slice(hh * I_MOE, (hh + 1) * I_MOE)
        return (swg_d[:, cs], swu_d[:, cs], swd_d[cs, :],
                swgb_d[hh], swub_d[hh], swdb_d[hh], None)

    t_p2_pe_done = c.tok(s_pe)
    t_l2w_all = (s_l2w, 16 * NSTRIP)
    rr = lambda ap: ap.rearrange("(c p) i -> p c i", p=128)
    rr_s = lambda t: t[:].rearrange("p (c i) -> p c i", c=NCH)

    t_gu_pe_done = None
    t_fin_done = None
    t_spills_g0 = None
    t_fin_roll = [None, None]
    t_out_roll = None
    n_out = 0
    t_outw = {}
    t_prev_u2 = None
    t_prev_silu2 = None
    t_prev_mult2 = None
    t_prev_d2 = None
    t_prev_ya_last = None
    t_wspill_g = {0: None, 1: None, 2: None}
    t_wspill_u = {0: None, 1: None, 2: None}
    t_dspill = {0: None, 1: None}
    t_slot_used = {0: None, 1: None, 2: None}
    gchunk = [0]

    for g in range(NG):
        w = [t_l2w_all, t_comb_done, t_p2_pe_done, t_gu_pe_done]
        src = ln2Ts_d[2 * g:2 * g + 2].rearrange("s c p f -> p s c f")
        t_lng = c.dma("sync", lnTg[:].rearrange("p (s c f) -> p s c f", s=GSTRIPS, c=NCH),
                      src, s_lng, waits=w)
        eorder = [16, 17] + list(range(16))
        for ei, e_ in enumerate(eorder):
            gsrc, usrc, dsrc, gbf, ubf, dbf, ce = esrc(e_)
            t_udone = {}
            t_silu_l = {}
            t_mult_l = {}
            nit = 0
            for ic in range(ICH):
                sl = gchunk[0] % 3
                gchunk[0] += 1
                w_ld = [t_slot_used[sl], t_wspill_g[sl], t_wspill_u[sl]]
                if (g, ei) == (0, 0) and ic < 2:
                    w_ld += [t_comb_done, t_l2w_all]
                if g == 1 and ei == 0 and ic < 2:
                    w_ld += t_spills_g0
                csl = slice(ic * 128, (ic + 1) * 128)
                if g == 0:
                    t_gl = c.dma("gpsimd", rr_s(wgq[sl]), rr(gsrc[:, csl]), s_wgc[sl], waits=w_ld)
                    t_ul = c.dma("gpsimd", rr_s(wuq[sl]), rr(usrc[:, csl]), s_wuc[sl])
                    t_wspill_g[sl] = c.dma("scalar", gbf[ic], wgq[sl][:], s_sg[sl],
                                           waits=[t_gl])
                    t_wspill_u[sl] = c.dma("scalar", ubf[ic], wuq[sl][:], s_su[sl],
                                           waits=[t_ul])
                else:
                    t_gl = c.dma("sync", wgq[sl][:], gbf[ic], s_wg[sl], waits=w_ld)
                    t_ul = c.dma("sync", wuq[sl][:], ubf[ic], s_wu[sl])
                for s in range(GSTRIPS):
                    ssl = nit % 2
                    w = [t_gl, t_lng,
                         t_silu_l.get(nit - 1) if nit >= 1 else t_prev_silu2]
                    for ci in range(NCH):
                        c.emit("tensor", lambda e, ci=ci, s=s, sl=sl: e.matmul(
                            ps_a[:], wgq[sl][:, ci * 128:(ci + 1) * 128],
                            lnTg[:, (s * NCH + ci) * 512:(s * NCH + ci) * 512 + 512],
                            start=(ci == 0), stop=(ci == NCH - 1)),
                            waits=w if ci == 0 else [], inc=(s_pe if ci == NCH - 1 else None))
                    t_gmm = c.tok(s_pe)
                    t_silu_l[nit] = c.emit(
                        "scalar", lambda e, ssl=ssl: e.activation(sgb[ssl][:], ps_a[:], AF.Silu),
                        waits=[t_gmm,
                               t_mult_l.get(nit - 2) if nit >= 2 else t_prev_mult2],
                        inc=s_act)
                    w = [t_ul, t_mult_l.get(nit - 1) if nit >= 1 else t_prev_mult2]
                    for ci in range(NCH):
                        c.emit("tensor", lambda e, ci=ci, s=s, sl=sl: e.matmul(
                            ps_b[:], wuq[sl][:, ci * 128:(ci + 1) * 128],
                            lnTg[:, (s * NCH + ci) * 512:(s * NCH + ci) * 512 + 512],
                            start=(ci == 0), stop=(ci == NCH - 1)),
                            waits=w if ci == 0 else [], inc=(s_pe if ci == NCH - 1 else None))
                    t_umm = c.tok(s_pe)
                    t_mult_l[nit] = c.emit(
                        "vector", lambda e, ic=ic, s=s, ssl=ssl: e.tensor_tensor(
                            ae[:, ic * GT + s * 512:ic * GT + s * 512 + 512],
                            sgb[ssl][:], ps_b[:], ALU.mult),
                        waits=[t_umm, t_silu_l[nit]], inc=s_dve)
                    if s == GSTRIPS - 1:
                        t_udone[ic] = t_mult_l[nit]
                        t_slot_used[sl] = t_mult_l[nit]
                    nit += 1
            t_prev_u2 = t_udone[ICH - 1]
            t_prev_silu2 = t_silu_l[nit - 1]
            t_prev_mult2 = t_mult_l[nit - 1]
            t_ae_done = t_udone[ICH - 1]
            t_dstop = {}
            t_ya = {}
            for dd in range(4):
                sl = dd % 2
                w_ld = [t_dstop.get(dd - 2) if dd >= 2 else t_prev_d2,
                        t_dspill[sl]]
                if (g, ei) == (0, 0) and dd < 2:
                    w_ld += [t_comb_done, t_l2w_all]
                if g == 1 and ei == 0 and dd < 2:
                    w_ld += t_spills_g0
                dsl = slice(dd * 512, (dd + 1) * 512)
                rrd = lambda ap: ap.rearrange("(c p) d -> p c d", p=128)
                rrd_s = lambda t: t[:].rearrange("p (c d) -> p c d", c=ICH)
                if g == 0:
                    t_dl = c.dma("gpsimd", rrd_s(wdq[sl]), rrd(dsrc[:, dsl]), s_wdc[sl], waits=w_ld)
                    t_dspill[sl] = c.dma("scalar", dbf[dd], wdq[sl][:], s_sd[sl],
                                         waits=[t_dl])
                else:
                    t_dl = c.dma("scalar", wdq[sl][:], dbf[dd], s_wd[sl], waits=w_ld)
                for sub in range(GSUB):
                    k = dd * GSUB + sub
                    bank = ps_c if k % 2 == 0 else ps_d
                    if k >= 2:
                        w = [t_dl, t_ae_done, t_ya[k - 2]]
                    elif ei > 0:
                        w = [t_dl, t_ae_done, t_prev_ya_last]
                    else:
                        w = [t_dl, t_ae_done, t_fin_done]
                    for ic in range(ICH):
                        c.emit("tensor", lambda e, b=bank, ic=ic, sub=sub, sl=sl: e.matmul(
                            b[:], ae[:, ic * GT + sub * 128:ic * GT + sub * 128 + 128],
                            wdq[sl][:, ic * 512:(ic + 1) * 512],
                            start=(ic == 0), stop=(ic == ICH - 1)),
                            waits=w if ic == 0 else [], inc=(s_pe if ic == ICH - 1 else None))
                    t_dstop[dd] = c.tok(s_pe)
                    ysl = yg[:, sub * D + dd * 512:sub * D + dd * 512 + 512]
                    if ei == 0:
                        t_ya[k] = c.emit(
                            "vector", lambda e, b=bank, ysl=ysl: e.tensor_copy(ysl, b[:]),
                            waits=[t_dstop[dd]], inc=s_dve)
                    elif ce is None:
                        t_ya[k] = c.emit(
                            "vector", lambda e, b=bank, ysl=ysl: e.scalar_tensor_tensor(
                                ysl, b[:], 1.0, ysl, ALU.mult, ALU.add),
                            waits=[t_dstop[dd]], inc=s_dve)
                    else:
                        ccol = comb[:, (g * GSUB + sub) * 16 + ce:(g * GSUB + sub) * 16 + ce + 1]
                        t_ya[k] = c.emit(
                            "vector", lambda e, b=bank, ysl=ysl, ccol=ccol: e.scalar_tensor_tensor(
                                ysl, b[:], ccol, ysl, ALU.mult, ALU.add),
                            waits=[t_dstop[dd]], inc=s_dve)
            t_prev_d2 = t_dstop[3]
            t_prev_ya_last = t_ya[4 * GSUB - 1]
            if g == 0 and ei == len(eorder) - 1:
                t_spills_g0 = [c.tok(s) for s in (s_sg[0], s_sg[1], s_sg[2],
                                                  s_su[0], s_su[1], s_su[2],
                                                  s_sd[0], s_sd[1])]
        t_gu_pe_done = t_dstop[3]
        t_y_done = t_ya[4 * GSUB - 1]

        for sub in range(GSUB):
            sl = 0
            ti = g * GSUB + sub
            t_xfl = c.dma("sync", xf[sl][:], x2s_d[ti], s_xf[sl],
                          waits=[(s_x2w, 16 * NTILE), t_y_done, t_fin_roll[sl]])
            ysl = yg[:, sub * D:(sub + 1) * D]
            t_r = ln_stats(ysl, [t_y_done], dump3)
            t_n = c.emit("vector", lambda e, ysl=ysl: e.tensor_scalar(
                ysl, ysl, stats[:, 2:3], stats[:, 5:6], ALU.subtract, ALU.mult),
                waits=[t_r], inc=s_dvs)
            t_fin_roll[sl] = c.emit("vector", lambda e, ysl=ysl, sl=sl: e.tensor_tensor(
                outf[:], ysl, xf[sl][:], ALU.add),
                waits=[t_n, t_xfl, t_out_roll], inc=s_dve)
            t_out_roll = c.dma("scalar", out_d[ti], outf[:], s_ow,
                               waits=[t_fin_roll[sl]])
            t_outw[n_out] = t_out_roll
            n_out += 1
        t_fin_done = t_fin_roll[0]

    final = [c.tok(s) for s in (s_ow, s_x2w, s_l2w, s_zw, s_sg[0], s_sg[1], s_sg[2],
                                 s_su[0], s_su[1], s_su[2], s_sd[0], s_sd[1], s_dbg)]
    c.wait("sync", final)

    with nc.Block() as block:
        @block.tensor
        def _(e):
            for f in c.q["tensor"]:
                f(e)

        @block.vector
        def _(e):
            for f in c.q["vector"]:
                f(e)

        @block.scalar
        def _(e):
            for f in c.q["scalar"]:
                f(e)

        @block.gpsimd
        def _(e):
            for f in c.q["gpsimd"]:
                f(e)

        @block.sync
        def _(e):
            for f in c.q["sync"]:
                f(e)

    return nc


def host_consts():
    masks = np.zeros((128, 4 * 512), np.float32)
    for j in range(4):
        u = np.arange(128)[:, None]
        v = np.arange(512)[None, :]
        masks[:, j * 512:(j + 1) * 512] = np.where(v >= u + 128 * j, 0.0, -1e9)
    return {
        "masks": masks.astype(ml_dtypes.bfloat16),
        "ident": np.eye(128, dtype=ml_dtypes.bfloat16),
        "onesb": np.ones((128, 1), ml_dtypes.bfloat16),
        "rowones": np.ones((1, 128), np.float32),
        "ccols": np.tile(np.array([[EPS, -SHIFT]], np.float32), (128, 1)),
    }


def kernel(L0_attn_x, L0_attn_W_K, L0_attn_W_Q, L0_attn_W_V, L0_attn_W_O,
           L0_ffn_gate_w, L0_ffn_w_gate, L0_ffn_w_up, L0_ffn_w_down,
           L0_ffn_sw_gate, L0_ffn_sw_up, L0_ffn_sw_down):
    xs = np.ascontiguousarray(np.asarray(L0_attn_x, np.float32))
    shared = {
        "wqT": np.ascontiguousarray(np.asarray(L0_attn_W_Q, np.float32).transpose(0, 2, 1)),
        "wkT": np.ascontiguousarray(np.asarray(L0_attn_W_K, np.float32).transpose(0, 2, 1)),
        "wvT": np.ascontiguousarray(np.asarray(L0_attn_W_V, np.float32).transpose(0, 2, 1)),
        "woT": np.ascontiguousarray(np.asarray(L0_attn_W_O, np.float32).T),
        "gwT": np.ascontiguousarray(np.asarray(L0_ffn_gate_w, np.float32).T),
        "wg": np.ascontiguousarray(np.asarray(L0_ffn_w_gate, np.float32)),
        "wu": np.ascontiguousarray(np.asarray(L0_ffn_w_up, np.float32)),
        "wd": np.ascontiguousarray(np.asarray(L0_ffn_w_down, np.float32)),
        "swg": np.ascontiguousarray(np.asarray(L0_ffn_sw_gate, np.float32)),
        "swu": np.ascontiguousarray(np.asarray(L0_ffn_sw_up, np.float32)),
        "swd": np.ascontiguousarray(np.asarray(L0_ffn_sw_down, np.float32)),
        **host_consts(),
    }
    nc = build()
    in_maps = [{"x": xs[b], **shared} for b in range(8)]
    res = run_bass_kernel_spmd(nc, in_maps, core_ids=list(range(8)))
    out = np.stack([np.asarray(res.results[b]["out"], np.float32).reshape(S, D)
                    for b in range(8)])
    return out
